# version 16
# baseline (speedup 1.0000x reference)
import numpy as np
import ml_dtypes
import concourse.bass as bass
import concourse.mybir as mybir
from concourse.bass_utils import run_bass_kernel_spmd

F32 = mybir.dt.float32
BF16 = mybir.dt.bfloat16
AX = mybir.AxisListType.X
AF = mybir.ActivationFunctionType
ALU = mybir.AluOpType

D = 1024
SQ = 2048
NH = 16
DK = 64
NE = 8
HID = 4096
NCORES = 8
NB = 2
DC = D // 128
NKC = SQ // 128
EPS = 1e-5

_STAGE = 4


class Prog:
    SECTION = {"pe": "tensor", "act": "scalar", "dve": "vector",
               "pool": "gpsimd", "sp": "sync"}

    def __init__(self, nc):
        self.nc = nc
        names = ["pe", "act", "dve", "pool", "sp"]
        self.sems = {s: nc.alloc_semaphore("sem_" + s) for s in names}
        self.count = {s: 0 for s in names}
        self.items = {e: [] for e in Prog.SECTION}
        self.last_w = {}
        self.readers = {}
        self.waited = {e: {} for e in Prog.SECTION}

    def op(self, eng, fn, reads=(), writes=(), dma=False, dslot=None):
        if dma:
            assert dslot is not None
            sem, inc = "D_" + dslot, 16
            if sem not in self.sems:
                self.sems[sem] = self.nc.alloc_semaphore("sem_" + sem)
                self.count[sem] = 0
            reads = list(reads) + [f"@{dslot}"]
            writes = list(writes) + [f"@{dslot}"]
        else:
            sem, inc = eng, 1
        need = {}

        def add(s, v):
            if s == eng and not dma and eng == "pe":
                return
            if v > need.get(s, 0):
                need[s] = v

        for r in reads:
            lw = self.last_w.get(r)
            if lw:
                add(*lw)
        for r in writes:
            lw = self.last_w.get(r)
            if lw:
                add(*lw)
            for s, v in self.readers.get(r, {}).items():
                add(s, v)
        wt = self.waited[eng]
        waits = [(s, v) for s, v in need.items() if wt.get(s, 0) < v]
        for s, v in waits:
            wt[s] = v
        self.count[sem] += inc
        val = self.count[sem]
        self.items[eng].append((waits, fn, sem, inc))
        for r in writes:
            self.last_w[r] = (sem, val)
            self.readers[r] = {}
        for r in reads:
            d = self.readers.setdefault(r, {})
            if val > d.get(sem, 0):
                d[sem] = val

    def finish(self):
        nc = self.nc
        final = [(s, self.count[s]) for s in self.sems
                 if s.startswith("D_") and self.count[s] > 0]
        with nc.Block() as blk:
            for ename, sname in Prog.SECTION.items():
                items = self.items[ename]

                def make(ename, items):
                    def run(eng):
                        for waits, fn, sem, inc in items:
                            for s, v in waits:
                                eng.wait_ge(self.sems[s], v)
                            fn(eng).then_inc(self.sems[sem], inc)
                        if ename == "sp":
                            for s, v in final:
                                eng.wait_ge(self.sems[s], v)
                    return run

                getattr(blk, sname)(make(ename, items))


def _fb(c0, c1):
    return [f"F{i}" for i in range(c0 // 512, (c1 + 511) // 512)]


def build_nc(stage=4):
    nc = bass.Bass("TRN2", target_bir_lowering=False)
    pr = Prog(nc)

    dx = nc.dram_tensor("x", [NB, SQ, D], F32, kind="ExternalInput").ap()
    dWqT = nc.dram_tensor("aWqT", [D, D], F32, kind="ExternalInput").ap()
    dWkT = nc.dram_tensor("aWkT", [D, D], F32, kind="ExternalInput").ap()
    dWvT = nc.dram_tensor("aWvT", [D, D], F32, kind="ExternalInput").ap()
    dWoT = nc.dram_tensor("aWoT", [D, D], F32, kind="ExternalInput").ap()
    dlnw = nc.dram_tensor("blnw", [D], F32, kind="ExternalInput").ap()
    dlnb = nc.dram_tensor("blnb", [D], F32, kind="ExternalInput").ap()
    dbq = nc.dram_tensor("bwq", [NH, D, DK], F32, kind="ExternalInput").ap()
    dbk = nc.dram_tensor("bwk", [NH, D, DK], F32, kind="ExternalInput").ap()
    dbv = nc.dram_tensor("bwv", [NH, D, DK], F32, kind="ExternalInput").ap()
    dbo = nc.dram_tensor("bwo", [D, D], F32, kind="ExternalInput").ap()
    dgate = nc.dram_tensor("fwg", [NE, D], F32, kind="ExternalInput").ap()
    dwin = nc.dram_tensor("fwin", [NE, HID, D], F32, kind="ExternalInput").ap()
    dwoutT = nc.dram_tensor("fwoutT", [NE, HID, D], F32, kind="ExternalInput").ap()
    dT1 = nc.dram_tensor("tT1", [128, SQ], F32, kind="ExternalInput").ap()
    dT2 = nc.dram_tensor("tT2", [128, SQ], F32, kind="ExternalInput").ap()
    dMe = nc.dram_tensor("tMe", [128, 128], F32, kind="ExternalInput").ap()
    dMo = nc.dram_tensor("tMo", [128, 128], F32, kind="ExternalInput").ap()
    dout = nc.dram_tensor("out", [NB, SQ, D], F32, kind="ExternalOutput").ap()

    sWeT = nc.dram_tensor("sWeT", [NE, D, D], BF16).ap()

    A = nc.alloc_sbuf_tensor
    x = [A(f"x{c}", [128, SQ], F32).ap() for c in range(DC)]
    y = [A(f"y{c}", [128, SQ], BF16).ap() for c in range(DC)]
    u = [A(f"u{c}", [128, SQ], BF16).ap() for c in range(DC)]
    qk = [A(f"qk{i}", [128, SQ], BF16).ap() for i in range(3)]
    P0 = A("P0", [128, SQ], BF16).ap()
    P1 = A("P1", [128, SQ], BF16).ap()
    vtok = A("vtok", [128, NKC * 160], BF16).ap()
    ws = [A(f"ws{i}", [128, D], BF16).ap() for i in range(3)]
    wo2 = [A(f"wo2{i}", [128, D], BF16).ap() for i in range(2)]
    ropeT1 = A("ropeT1", [128, SQ], BF16).ap()
    ropeT2 = A("ropeT2", [128, SQ], BF16).ap()
    rMe = A("rMe", [128, 128], BF16).ap()
    rMo = A("rMo", [128, 128], BF16).ap()
    rowp = A("rowp", [66, SQ], F32).ap()
    rowsm = A("rowsm", [128, 16], F32).ap()
    bigtri = A("bigtri", [128, 640], BF16).ap()
    idf = A("idf", [128, 128], F32).ap()
    oneh = A("oneh", [8, NE * 128], BF16).ap()
    onescb = A("onescb", [128, 1], BF16).ap()
    ones1f = A("ones1f", [66, 128], F32).ap()
    wgT = A("wgT", [128, 64], BF16).ap()
    lnwb = A("lnwb", [128, 16], F32).ap()
    stag = A("stag", [128, 2048], F32).ap()
    tmpB = A("tmpB", [64, SQ], BF16).ap()
    gfm = A("gfm", [8, SQ], BF16).ap()
    lgt = A("lgt", [128, 32], F32).ap()
    gts = A("gts", [128, 32], F32).ap()
    lgs = A("lgs", [8, 512], F32).ap()
    gsc = A("gsc", [128, 48], F32).ap()
    Gs = A("Gs", [128, 512], BF16).ap()
    ring = [A(f"ring{i}", [128, 512], BF16).ap() for i in range(4)]
    _cvals = [1e-5, 1e-30, 0.5, -0.5, 0.125, -1.0]
    csts = A("csts", [128, len(_cvals)], F32).ap()
    for _i, _v in enumerate(_cvals):
        nc.const_aps.register(F32, _v, csts[:, _i:_i + 1])

    tmp = stag[:, 1024:1536]
    tmp2 = stag[:, 1536:2048]

    def yef_view(dc):
        t = P0 if dc < 4 else P1
        c0 = (dc % 4) * 512
        return t[:, c0:c0 + 512], ("P0" if dc < 4 else "P1")

    pF = nc.alloc_psum_tensor("pF", [128, 4096], F32).ap()

    def dmaop(q, fn, dslot, reads=(), writes=()):
        pr.op(q, fn, reads=reads, writes=writes, dma=True, dslot=dslot)

    def gen_consts():
        g = pr.op
        g("pool", lambda e: e.memset(idf, 0.0), writes=["idf"])
        g("pool", lambda e: e.affine_select(out=idf, in_=idf,
          compare_op=ALU.not_equal, fill=1.0, base=0, pattern=[[-1, 128]],
          channel_multiplier=1), reads=["idf"], writes=["idf"])
        g("pool", lambda e: e.memset(bigtri, 1.0), writes=["tri"])
        g("pool", lambda e: e.affine_select(out=bigtri, in_=bigtri,
          compare_op=ALU.is_ge, fill=0.0, base=-512, pattern=[[1, 640]],
          channel_multiplier=-1), reads=["tri"], writes=["tri"])
        g("pool", lambda e: e.memset(oneh, 0.0), writes=["oneh"])
        for e_ in range(NE):
            g("pool", lambda e, e_=e_: e.affine_select(
                out=oneh[:, e_ * 128:(e_ + 1) * 128],
                in_=oneh[:, e_ * 128:(e_ + 1) * 128],
                compare_op=ALU.not_equal, fill=1.0, base=-e_,
                pattern=[[0, 128]], channel_multiplier=1),
              reads=["oneh"], writes=["oneh"])
        g("pool", lambda e: e.memset(onescb, 1.0), writes=["onescb"])
        g("pool", lambda e: e.memset(ones1f, 1.0), writes=["ones1f"])
        g("pool", lambda e: e.memset(vtok, 1.0), writes=["vtok"])
        for _i, _v in enumerate(_cvals):
            g("pool", lambda e, _i=_i, _v=_v: e.memset(csts[:, _i:_i + 1], _v),
              writes=["csts"])
        dmaop("pool", lambda e: e.dma_start(ropeT1, dT1), "consts", writes=["ropeT1"])
        dmaop("pool", lambda e: e.dma_start(ropeT2, dT2), "consts", writes=["ropeT2"])
        dmaop("pool", lambda e: e.dma_start(rMe, dMe), "consts", writes=["rMe"])
        dmaop("pool", lambda e: e.dma_start(rMo, dMo), "consts", writes=["rMo"])
        lw = dlnw.rearrange("(c p one) -> c p one", p=128, one=1)
        lb = dlnb.rearrange("(c p one) -> c p one", p=128, one=1)
        for c in range(DC):
            dmaop("sp", lambda e, c=c: e.dma_start(lnwb[:, c:c + 1], lw[c]),
                  "lnwb", writes=["lnwb"])
            dmaop("sp", lambda e, c=c: e.dma_start(lnwb[:, 8 + c:9 + c], lb[c]),
                  "lnwb", writes=["lnwb"])

    def prep_gate():
        dmaop("sp", lambda e: e.dma_start(stag[0:8, 0:D], dgate), "lnwb",
              writes=["stag0", "stag1"])
        for c in range(DC):
            pr.op("pe", lambda e, c=c: e.transpose(
                pF[:, 3584:3592], stag[0:8, c * 128:(c + 1) * 128],
                idf[0:8, 0:8]),
                reads=["stag0", "stag1", "idf", "F7"], writes=["F7"])
            pr.op("dve", lambda e, c=c: e.tensor_copy(
                wgT[:, c * 8:(c + 1) * 8], pF[:, 3584:3592]),
                reads=["F7"], writes=["wgT"])

    def prep_ffn():
        dwv = dwin.rearrange("e (hg p) d -> e p hg d", p=128)
        dov = dwoutT.rearrange("e (hg p) j -> e p hg j", p=128)
        for e_ in range(NE):
            st_names = []
            for jb in range(2):
                for dcg in range(2):
                    for hg in range(8):
                        wbl = [qk[0], qk[1], P1, u[0]]
                        wbns = ["pqk0", "pqk1", "pP1x", "pu0"]
                        wb, wbn = wbl[hg % 4], wbns[hg % 4]
                        obl = [qk[2], P0, u[1], u[2]]
                        obnl = ["pqk2", "pP0x", "pu1", "pu2"]
                        ob, obn = obl[hg % 4], obnl[hg % 4]
                        obns = [obn]
                        dmaop("pool",
                              lambda e, e_=e_, hg=hg, dcg=dcg, wb=wb:
                              e.dma_start(
                                  wb.rearrange("p (hc c) -> p hc c", hc=4),
                                  dwv[e_][:, hg * 4:(hg + 1) * 4,
                                          dcg * 512:(dcg + 1) * 512]),
                              wbn, reads=[], writes=[wbn])
                        dmaop("pool",
                              lambda e, e_=e_, hg=hg, jb=jb, ob=ob:
                              e.dma_start(
                                  ob.rearrange("p (hc c) -> p hc c", hc=4),
                                  dov[e_][:, hg * 4:(hg + 1) * 4,
                                          jb * 512:(jb + 1) * 512]),
                              obn, reads=[], writes=obns)
                        for hc4 in range(4):
                            for dci in range(4):
                                pr.op("pe", lambda e, wb=wb, ob=ob, dci=dci,
                                      hg=hg, hc4=hc4:
                                      e.matmul(pF[:, dci * 512:(dci + 1) * 512],
                                               wb[:, hc4 * 512 + dci * 128:
                                                  hc4 * 512 + (dci + 1) * 128],
                                               ob[:, hc4 * 512:(hc4 + 1) * 512],
                                               start=(hg == 0 and hc4 == 0),
                                               stop=(hg == 7 and hc4 == 3)),
                                      reads=[wbn] + obns, writes=[f"F{dci}"])
                    for dci in range(4):
                        dc_ = dcg * 4 + dci
                        sb, sbn = wo2[dci % 2], f"wo2{dci % 2}"
                        pr.op("dve", lambda e, dci=dci, sb=sb: e.tensor_copy(
                            sb[:, 0:512], pF[:, dci * 512:(dci + 1) * 512]),
                            reads=[f"F{dci}"], writes=[sbn])
                        nm = f"sWeT{e_}.{jb}.{dc_}"
                        st_names.append(nm)
                        dmaop("sp", lambda e, e_=e_, dc_=dc_, jb=jb, sb=sb:
                              e.dma_start(sWeT[e_][dc_ * 128:(dc_ + 1) * 128,
                                                   jb * 512:(jb + 1) * 512],
                                          sb[:, 0:512]),
                              f"stw{dci % 2}", reads=[sbn], writes=[nm])
            pr.op("pool", lambda e: e.nop(),
                  reads=st_names, writes=[f"sWeT{e_}"])

    def x_in(b):
        for t in range(SQ // 128):
            sg = stag[:, (t % 2) * 1024:(t % 2) * 1024 + 1024]
            sgn = ["stag0"] if t % 2 == 0 else ["stag1", "stag2"]
            dmaop("sp", lambda e, b=b, t=t, sg=sg: e.dma_start(
                sg, dx[b, t * 128:(t + 1) * 128, :]), f"stg{t % 2}",
                writes=sgn)
            for c in range(DC):
                pc = pF[:, 2048 + (c % 4) * 256:2048 + (c % 4) * 256 + 128]
                bank = f"F{4 + (c % 4) // 2}"
                pr.op("pe", lambda e, c=c, sg=sg, pc=pc: e.transpose(
                    pc, sg[:, c * 128:(c + 1) * 128], idf),
                    reads=sgn + ["idf", bank], writes=[bank])
                pr.op("dve", lambda e, c=c, t=t, pc=pc: e.tensor_copy(
                    x[c][:, t * 128:(t + 1) * 128], pc),
                    reads=[bank], writes=[f"x{c}"])

    def x_out(b):
        for t in range(SQ // 128):
            sg = stag[:, (t % 2) * 1024:(t % 2) * 1024 + 1024]
            sgn = ["stag0"] if t % 2 == 0 else ["stag1", "stag2"]
            for c in range(DC):
                pc = pF[:, 2048 + (c % 4) * 256:2048 + (c % 4) * 256 + 128]
                bank = f"F{4 + (c % 4) // 2}"
                pr.op("pe", lambda e, c=c, t=t, pc=pc: e.transpose(
                    pc, x[c][:, t * 128:(t + 1) * 128], idf),
                    reads=[f"x{c}", "idf", bank], writes=[bank])
                pr.op("dve", lambda e, c=c, sg=sg, pc=pc: e.tensor_copy(
                    sg[:, c * 128:(c + 1) * 128], pc),
                    reads=[bank], writes=sgn)
            dmaop("sp", lambda e, b=b, t=t, sg=sg: e.dma_start(
                dout[b, t * 128:(t + 1) * 128, :], sg),
                f"stg{t % 2}", reads=sgn, writes=[f"dout{b}.{t}"])

    def ln(src_is_y=False, fp32_res=False, affine=False):
        src = y if src_is_y else x
        sn = "y" if src_is_y else "x"
        s1v = pF[0:1, 2048:2560]
        s2v = pF[0:1, 2560:3072]
        for t in range(4):
            for c in range(DC):
                if src_is_y:
                    xb, xbn = src[c][:, t * 512:(t + 1) * 512], f"y{c}"
                else:
                    xb, xbn = ring[c % 2], f"ring{c % 2}"
                    pr.op("pool", lambda e, c=c, t=t, xb=xb: e.tensor_copy(
                        xb, x[c][:, t * 512:(t + 1) * 512]),
                        reads=[f"x{c}"], writes=[xbn])
                sq, sqn = (P0 if c % 2 == 0 else P1), ("P0.0" if c % 2 == 0 else "P1.0")
                sq = sq[:, 0:512]
                pr.op("act", lambda e, c=c, t=t, sq=sq: e.activation(
                    sq, src[c][:, t * 512:(t + 1) * 512], AF.Square),
                    reads=[f"{sn}{c}"], writes=[sqn])
                pr.op("pe", lambda e, c=c, xb=xb: e.matmul(
                    s1v, onescb, xb, start=(c == 0), stop=(c == DC - 1)),
                    reads=[xbn, "onescb", "F4"], writes=["F4"])
                pr.op("pe", lambda e, c=c, sq=sq: e.matmul(
                    s2v, onescb, sq, start=(c == 0), stop=(c == DC - 1)),
                    reads=[sqn, "onescb", "F5"], writes=["F5"])
            pr.op("dve", lambda e, t=t: e.tensor_copy(
                rowp[0:1, t * 512:(t + 1) * 512], s1v),
                reads=["F4"], writes=["row0"])
            pr.op("dve", lambda e, t=t: e.tensor_copy(
                rowp[32:33, t * 512:(t + 1) * 512], s2v),
                reads=["F5"], writes=["row32"])
        pr.op("dve", lambda e: e.tensor_scalar(
            rowp[0:1, :], rowp[0:1, :], -1.0 / D, None, ALU.mult),
            reads=["row0"], writes=["row0"])
        for t in range(4):
            pr.op("dve", lambda e, t=t: e.tensor_tensor(
                pF[0:1, 3072:3584], rowp[0:1, t * 512:(t + 1) * 512],
                rowp[0:1, t * 512:(t + 1) * 512], ALU.mult),
                reads=["row0", "F6"], writes=["F6"])
            pr.op("dve", lambda e, t=t: e.scalar_tensor_tensor(
                rowp[32:33, t * 512:(t + 1) * 512],
                rowp[32:33, t * 512:(t + 1) * 512], 1.0 / D,
                pF[0:1, 3072:3584], ALU.mult, ALU.subtract),
                reads=["row32", "F6"], writes=["row32"])
        pr.op("act", lambda e: e.activation(
            rowp[32:33, :], rowp[32:33, :], AF.Ln, bias=EPS),
            reads=["row32", "csts"], writes=["row32"])
        pr.op("act", lambda e: e.activation(
            rowp[32:33, :], rowp[32:33, :], AF.Exp, scale=-0.5),
            reads=["row32", "csts"], writes=["row32"])
        for t in range(4):
            pr.op("pe", lambda e, t=t: e.matmul(
                pF[:, 2048:2560], ones1f[0:1, :], rowp[0:1, t * 512:(t + 1) * 512],
                start=True, stop=True),
                reads=["row0", "ones1f", "F4"], writes=["F4"])
            pr.op("pe", lambda e, t=t: e.matmul(
                pF[:, 2560:3072], ones1f[32:33, :], rowp[32:33, t * 512:(t + 1) * 512],
                start=True, stop=True),
                reads=["row32", "ones1f", "F5"], writes=["F5"])
            for c in range(DC):
                sl = (t * 512, (t + 1) * 512)
                pr.op("dve", lambda e, c=c, sl=sl: e.tensor_tensor(
                    tmp, src[c][:, sl[0]:sl[1]], pF[:, 2048:2560], ALU.add),
                    reads=[f"{sn}{c}", "F4"], writes=["stag1"])
                if affine:
                    pr.op("dve", lambda e, c=c, sl=sl: e.tensor_tensor(
                        tmp2, tmp, pF[:, 2560:3072], ALU.mult),
                        reads=["stag1", "F5"], writes=["stag2"])
                    pr.op("dve", lambda e, c=c, sl=sl: e.tensor_scalar(
                        y[c][:, sl[0]:sl[1]], tmp2,
                        lnwb[:, c:c + 1], lnwb[:, 8 + c:9 + c],
                        ALU.mult, ALU.add),
                        reads=["stag2", "lnwb"], writes=[f"y{c}"])
                else:
                    pr.op("dve", lambda e, c=c, sl=sl: e.tensor_tensor(
                        y[c][:, sl[0]:sl[1]], tmp, pF[:, 2560:3072], ALU.mult),
                        reads=["stag1", "F5"], writes=[f"y{c}"])
                    if fp32_res:
                        pr.op("dve", lambda e, c=c, sl=sl: e.tensor_tensor(
                            tmp2, tmp, pF[:, 2560:3072], ALU.mult),
                            reads=["stag1", "F5"], writes=["stag2"])
                        pr.op("dve", lambda e, c=c, sl=sl: e.tensor_tensor(
                            x[c][:, sl[0]:sl[1]], x[c][:, sl[0]:sl[1]],
                            tmp2, ALU.add),
                            reads=[f"x{c}", "stag2"], writes=[f"x{c}"])

    def attention(layer):
        for hp in range(NH // 2):
            if layer == "a":
                for ti, src in enumerate((dWqT, dWkT, dWvT)):
                    sv = src.rearrange("(dc p) j -> p dc j", p=128)
                    dvw = ws[ti].rearrange("p (dc j) -> p dc j", j=128)
                    dmaop("pool", lambda e, sv=sv, dvw=dvw, hp=hp: e.dma_start(
                        dvw, sv[:, :, hp * 128:(hp + 1) * 128]),
                        f"ws{ti}", reads=[], writes=[f"ws{ti}"])
            else:
                for ti, src in ((0, dbq), (1, dbk), (2, dbv)):
                    for h2 in range(2):
                        h = 2 * hp + h2
                        sv = src[h].rearrange("(dc p) e -> p dc e", p=128)
                        dvw = ws[ti].rearrange("p (dc he) -> p dc he", he=128)
                        dmaop("pool", lambda e, sv=sv, dvw=dvw, h2=h2:
                              e.dma_start(dvw[:, :, h2 * 64:(h2 + 1) * 64], sv),
                              f"ws{ti}", reads=[], writes=[f"ws{ti}"])
            for ti in range(3):
                po = (ti % 2) * 2048
                for dc in range(DC):
                    for t in range(4):
                        pr.op("pe", lambda e, ti=ti, dc=dc, t=t, po=po:
                              e.matmul(
                                  pF[:, po + t * 512:po + (t + 1) * 512],
                                  ws[ti][:, dc * 128:(dc + 1) * 128],
                                  y[dc][:, t * 512:(t + 1) * 512],
                                  start=(dc == 0), stop=(dc == DC - 1)),
                              reads=[f"ws{ti}", f"y{dc}", f"F{po // 512 + t}"],
                              writes=[f"F{po // 512 + t}"])
                for t in range(4):
                    pr.op("dve", lambda e, ti=ti, t=t, po=po: e.tensor_copy(
                        qk[ti][:, t * 512:(t + 1) * 512],
                        pF[:, po + t * 512:po + (t + 1) * 512]),
                        reads=[f"F{po // 512 + t}"], writes=[f"qk{ti}"])
            if layer == "a":
                for ti in range(2):
                    po = ((ti + 1) % 2) * 2048
                    pr.op("dve", lambda e, ti=ti: e.tensor_tensor(
                        P0, qk[ti], ropeT1, ALU.mult),
                        reads=[f"qk{ti}", "ropeT1"],
                        writes=[f"P0.{i}" for i in range(4)])
                    pr.op("dve", lambda e, ti=ti: e.tensor_tensor(
                        P1, qk[ti], ropeT2, ALU.mult),
                        reads=[f"qk{ti}", "ropeT2"],
                        writes=[f"P1.{i}" for i in range(4)])
                    for t in range(4):
                        pr.op("pe", lambda e, t=t, po=po: e.matmul(
                            pF[:, po + t * 512:po + (t + 1) * 512], rMe,
                            P0[:, t * 512:(t + 1) * 512],
                            start=True, stop=False),
                            reads=["rMe", f"P0.{t}", f"F{po // 512 + t}"],
                            writes=[f"F{po // 512 + t}"])
                        pr.op("pe", lambda e, t=t, po=po: e.matmul(
                            pF[:, po + t * 512:po + (t + 1) * 512], rMo,
                            P1[:, t * 512:(t + 1) * 512],
                            start=False, stop=True),
                            reads=["rMo", f"P1.{t}", f"F{po // 512 + t}"],
                            writes=[f"F{po // 512 + t}"])
                    for t in range(4):
                        pr.op("dve", lambda e, ti=ti, t=t, po=po: e.tensor_copy(
                            qk[ti][:, t * 512:(t + 1) * 512],
                            pF[:, po + t * 512:po + (t + 1) * 512]),
                            reads=[f"F{po // 512 + t}"], writes=[f"qk{ti}"])
            for kc in range(NKC):
                dmaop("sp", lambda e, kc=kc: e.dma_start(
                    vtok[:, kc * 160:kc * 160 + 64],
                    qk[2][0:64, kc * 128:(kc + 1) * 128], transpose=True),
                    f"vtA{kc % 2}", reads=["qk2"], writes=["vtok"])
                dmaop("act", lambda e, kc=kc: e.dma_start(
                    vtok[:, kc * 160 + 80:kc * 160 + 144],
                    qk[2][64:128, kc * 128:(kc + 1) * 128], transpose=True),
                    f"vtB{kc % 2}", reads=["qk2"], writes=["vtok"])
            for ti in (1, 0):
                for h2 in range(2):
                    hb = 64 * h2
                    rowr = rowp[0:1, :] if h2 == 0 else rowp[32:33, :]
                    rown = "row0" if h2 == 0 else "row32"
                    pr.op("act", lambda e, ti=ti, hb=hb, hp=hp: e.activation(
                        u[hp][hb:hb + 64, 0:SQ], qk[ti][hb:hb + 64, :],
                        AF.Square), reads=[f"qk{ti}"], writes=[f"u{hp}"])
                    for t in range(4):
                        po = 3072 + ((t + h2) % 2) * 512
                        pb = f"F{po // 512}"
                        pr.op("pe", lambda e, t=t, hb=hb, hp=hp, po=po:
                              e.matmul(pF[0:1, po:po + 512],
                                       onescb[hb:hb + 64, :],
                                       u[hp][hb:hb + 64, t * 512:(t + 1) * 512],
                                       start=True, stop=True),
                              reads=[f"u{hp}", "onescb", pb], writes=[pb])
                        pr.op("dve", lambda e, t=t, rowr=rowr, po=po:
                              e.tensor_copy(rowr[:, t * 512:(t + 1) * 512],
                                            pF[0:1, po:po + 512]),
                              reads=[pb], writes=[rown])
                    col = h2 * 8 + (0 if ti == 1 else 1)
                    pr.op("dve", lambda e, col=col, rowr=rowr: e.reduce_max(
                        rowsm[0:1, col:col + 1], rowr, axis=AX),
                        reads=[rown], writes=["rowsm"])
            for h2 in range(2):
                cb = h2 * 8
                pr.op("dve", lambda e, cb=cb: e.tensor_tensor(
                    rowsm[0:1, cb + 2:cb + 3], rowsm[0:1, cb:cb + 1],
                    rowsm[0:1, cb + 1:cb + 2], ALU.mult),
                    reads=["rowsm"], writes=["rowsm"])
                pr.op("act", lambda e, cb=cb: e.activation(
                    rowsm[0:1, cb + 3:cb + 4], rowsm[0:1, cb + 2:cb + 3],
                    AF.Ln, bias=1e-30),
                    reads=["rowsm", "csts"], writes=["rowsm"])
                pr.op("act", lambda e, cb=cb: e.activation(
                    rowsm[0:1, cb + 3:cb + 4], rowsm[0:1, cb + 3:cb + 4],
                    AF.Exp, scale=0.5),
                    reads=["rowsm", "csts"], writes=["rowsm"])
                pr.op("dve", lambda e, cb=cb: e.tensor_scalar(
                    rowsm[0:1, cb + 4:cb + 5], rowsm[0:1, cb + 3:cb + 4],
                    -1.02 / 8.0, -1.0, ALU.mult, ALU.add),
                    reads=["rowsm"], writes=["rowsm"])
                po = 3072 + h2 * 512
                pb = f"F{po // 512}"
                pr.op("pe", lambda e, cb=cb, po=po: e.matmul(
                    pF[:, po:po + 1], ones1f[0:1, :], rowsm[0:1, cb + 4:cb + 5],
                    start=True, stop=True),
                    reads=["rowsm", "ones1f", pb], writes=[pb])
                pr.op("dve", lambda e, cb=cb, po=po: e.tensor_copy(
                    rowsm[:, cb + 5:cb + 6], pF[:, po:po + 1]),
                    reads=[pb], writes=["rowsm"])
            for qh in range(2):
                nkc = 8 if qh == 0 else NKC

                def emit_pv(h2, kc, qh=qh):
                    qbl0 = max(0, kc // 4 - 2 * qh)
                    P = P0 if h2 == 0 else P1
                    for qbl in range(qbl0, 2):
                        qb = 2 * qh + qbl
                        Pn = (("P0" if h2 == 0 else "P1") +
                              f".{2 * (kc % 2) + qbl}")
                        reg = (h2 * 2 + qbl) * 512
                        pr.op("pe", lambda e, kc=kc, qb=qb, h2=h2, P=P,
                              reg=reg, qbl=qbl, qh=qh:
                              e.matmul(pF[0:65, reg:reg + 512],
                                       vtok[:, kc * 160 + h2 * 80:
                                            kc * 160 + h2 * 80 + 65],
                                       P[:, (kc % 2) * 1024 + qbl * 512:
                                         (kc % 2) * 1024 + (qbl + 1) * 512],
                                       start=(kc == 0),
                                       stop=(kc == min(4 * qb + 3, nkc - 1))),
                              reads=["vtok", Pn, f"F{reg // 512}"],
                              writes=[f"F{reg // 512}"])

                pend = [None, None]
                for kc in range(nkc):
                    for h2 in range(2):
                        hb = 64 * h2
                        bias_ap = rowsm[:, h2 * 8 + 5:h2 * 8 + 6]
                        qbl0 = max(0, kc // 4 - 2 * qh)
                        P = P0 if h2 == 0 else P1
                        PN = "P0" if h2 == 0 else "P1"
                        stbase = 2048 + h2 * 1024
                        for qbl in range(qbl0, 2):
                            qb = 2 * qh + qbl
                            stoff = stbase + qbl * 512
                            stb = f"F{stoff // 512}"
                            pr.op("pe", lambda e, kc=kc, qb=qb, hb=hb,
                                  stoff=stoff:
                                  e.matmul(pF[:, stoff:stoff + 512],
                                           qk[1][hb:hb + 64,
                                                 kc * 128:(kc + 1) * 128],
                                           qk[0][hb:hb + 64,
                                                 qb * 512:(qb + 1) * 512],
                                           start=True, stop=True),
                                  reads=["qk1", "qk0", stb], writes=[stb])
                        w_ = 512 * (2 - qbl0)
                        stoff = stbase + qbl0 * 512
                        banks = [f"F{(stoff // 512) + i}" for i in range(2 - qbl0)]
                        pr.op("act", lambda e, P=P, kc=kc, qbl0=qbl0, w_=w_,
                              stoff=stoff, bias_ap=bias_ap:
                              e.activation(
                                  P[:, (kc % 2) * 1024 + qbl0 * 512:
                                    (kc % 2) * 1024 + qbl0 * 512 + w_],
                                  pF[:, stoff:stoff + w_],
                                  AF.Exp, scale=0.125, bias=bias_ap),
                              reads=banks + ["csts", "rowsm"],
                              writes=[f"{PN}.{2 * (kc % 2) + qbl0 + i}"
                                      for i in range(2 - qbl0)])
                        qbl_d = kc // 4 - 2 * qh
                        if 0 <= qbl_d <= 1:
                            o_ = 128 * (kc % 4)
                            mtok = f"{PN}.{2 * (kc % 2) + qbl_d}"
                            pr.op("dve", lambda e, P=P, kc=kc, qbl_d=qbl_d,
                                  o_=o_:
                                  e.tensor_tensor(
                                      P[:, (kc % 2) * 1024 + qbl_d * 512:
                                        (kc % 2) * 1024 + qbl_d * 512 + o_ + 128],
                                      P[:, (kc % 2) * 1024 + qbl_d * 512:
                                        (kc % 2) * 1024 + qbl_d * 512 + o_ + 128],
                                      bigtri[:, 512 - o_:640], ALU.mult),
                                  reads=[mtok, "tri"], writes=[mtok])
                    for h2 in range(2):
                        if pend[h2] is not None:
                            emit_pv(h2, pend[h2])
                        pend[h2] = kc
                for h2 in range(2):
                    emit_pv(h2, pend[h2])
                for h2 in range(2):
                    lr = (rowp[64:65, :] if h2 == 0 else rowp[32:33, :])
                    lr = lr[:, qh * 1024:(qh + 1) * 1024]
                    lrn = ("row64" if h2 == 0 else "row32")
                    lones = (ones1f[64:65, 0:64] if h2 == 0
                             else ones1f[32:33, 0:64])
                    lrow_psum = pF[64:65, h2 * 1024:h2 * 1024 + 1024]
                    fb2 = [f"F{h2 * 2}", f"F{h2 * 2 + 1}"]
                    pr.op("dve", lambda e, lr=lr, lrow_psum=lrow_psum:
                          e.tensor_copy(lr, lrow_psum),
                          reads=fb2, writes=[lrn])
                    pr.op("act", lambda e, lr=lr: e.activation(
                        lr, lr, AF.Ln, bias=1e-30),
                        reads=[lrn, "csts"], writes=[lrn])
                    pr.op("act", lambda e, lr=lr: e.activation(
                        lr, lr, AF.Exp, scale=-1.0),
                        reads=[lrn, "csts"], writes=[lrn])
                    bco = 2048 + h2 * 1024
                    bcb = f"F{bco // 512}"
                    for t2 in range(2):
                        tg = qh * 1024 + t2 * 512
                        pr.op("pe", lambda e, t2=t2, lr=lr, lones=lones,
                              bco=bco: e.matmul(
                                  pF[0:64, bco:bco + 512], lones,
                                  lr[:, t2 * 512:(t2 + 1) * 512],
                                  start=True, stop=True),
                              reads=[lrn, "ones1f", bcb], writes=[bcb])
                        reg = h2 * 1024 + t2 * 512
                        regb = f"F{reg // 512}"
                        if h2 == 0:
                            pr.op("dve", lambda e, tg=tg, hp=hp, reg=reg:
                                  e.tensor_copy(
                                      u[hp][0:64, tg:tg + 512],
                                      pF[0:64, reg:reg + 512]),
                                  reads=[regb], writes=[f"u{hp}"])
                            pr.op("dve", lambda e, tg=tg, hp=hp, bco=bco:
                                  e.tensor_tensor(
                                      u[hp][0:64, tg:tg + 512],
                                      u[hp][0:64, tg:tg + 512],
                                      pF[0:64, bco:bco + 512], ALU.mult),
                                  reads=[f"u{hp}", bcb], writes=[f"u{hp}"])
                        else:
                            pr.op("dve", lambda e, tg=tg, reg=reg:
                                  e.tensor_copy(
                                      tmpB[0:64, tg:tg + 512],
                                      pF[0:64, reg:reg + 512]),
                                  reads=[regb], writes=["tmpB"])
                            pr.op("dve", lambda e, tg=tg, bco=bco:
                                  e.tensor_tensor(
                                      tmpB[0:64, tg:tg + 512],
                                      tmpB[0:64, tg:tg + 512],
                                      pF[0:64, bco:bco + 512], ALU.mult),
                                  reads=["tmpB", bcb], writes=["tmpB"])
                            pr.op("dve", lambda e, tg=tg, hp=hp:
                                  e.tensor_copy(
                                      u[hp][64:128, tg:tg + 512],
                                      tmpB[0:64, tg:tg + 512]),
                                  reads=["tmpB"], writes=[f"u{hp}"])
        srcT = dWoT if layer == "a" else dbo
        for jc in range(DC):
            buf, bn = wo2[jc % 2], f"wo2{jc % 2}"
            sv = srcT.rearrange("(dc p) j -> p dc j", p=128)
            dvw = buf.rearrange("p (dc j) -> p dc j", j=128)
            dmaop("pool", lambda e, sv=sv, dvw=dvw, jc=jc: e.dma_start(
                dvw, sv[:, :, jc * 128:(jc + 1) * 128]),
                f"wol{jc % 2}", reads=[], writes=[bn])
            for dc in range(DC):
                for t in range(4):
                    pr.op("pe", lambda e, buf=buf, dc=dc, t=t: e.matmul(
                        pF[:, t * 512:(t + 1) * 512],
                        buf[:, dc * 128:(dc + 1) * 128],
                        u[dc][:, t * 512:(t + 1) * 512],
                        start=(dc == 0), stop=(dc == DC - 1)),
                        reads=[bn, f"u{dc}", f"F{t}"], writes=[f"F{t}"])
            for t in range(4):
                pr.op("dve", lambda e, jc=jc, t=t: e.tensor_tensor(
                    x[jc][:, t * 512:(t + 1) * 512],
                    x[jc][:, t * 512:(t + 1) * 512],
                    pF[:, t * 512:(t + 1) * 512], ALU.add),
                    reads=[f"x{jc}", f"F{t}"], writes=[f"x{jc}"])

    FFW = [(0, 512), (512, 512), (1024, 512), (1536, 512)]

    def ffn():
        for t in range(4):
            for dc in range(DC):
                pr.op("pe", lambda e, dc=dc, t=t: e.matmul(
                    pF[0:8, 3072:3584], wgT[:, dc * 8:(dc + 1) * 8],
                    y[dc][:, t * 512:(t + 1) * 512],
                    start=(dc == 0), stop=(dc == DC - 1)),
                    reads=[f"y{dc}", "wgT", "F6"], writes=["F6"])
            pr.op("act", lambda e, t=t: e.activation(
                lgs, pF[0:8, 3072:3584], AF.Copy),
                reads=["F6"], writes=["lgs"])
            for i in range(4):
                pr.op("pe", lambda e, i=i: e.transpose(
                    pF[:, 3584 + i * 8:3592 + i * 8],
                    lgs[0:8, i * 128:(i + 1) * 128], idf[0:8, 0:8]),
                    reads=["lgs", "idf", "F7"], writes=["F7"])
            pr.op("dve", lambda e, t=t: e.tensor_copy(
                lgt[:, 0:32], pF[:, 3584:3616]),
                reads=["F7"], writes=["lgt"])
            for i in range(4):
                lv = lgt[:, i * 8:(i + 1) * 8]
                pr.op("dve", lambda e, lv=lv: e.reduce_max(
                    gsc[:, 0:1], lv, axis=AX), reads=["lgt"], writes=["gsc"])
                pr.op("dve", lambda e, lv=lv: e.tensor_scalar(
                    gsc[:, 8:16], lv, gsc[:, 0:1], None, ALU.is_equal),
                    reads=["lgt", "gsc"], writes=["gsc"])
                pr.op("dve", lambda e, lv=lv: e.scalar_tensor_tensor(
                    gsc[:, 16:24], gsc[:, 8:16], -1e30, lv,
                    ALU.mult, ALU.add), reads=["gsc", "lgt"], writes=["gsc"])
                pr.op("dve", lambda e: e.reduce_max(
                    gsc[:, 1:2], gsc[:, 16:24], axis=AX),
                    reads=["gsc"], writes=["gsc"])
                pr.op("dve", lambda e: e.tensor_tensor(
                    gsc[:, 2:3], gsc[:, 0:1], gsc[:, 1:2], ALU.subtract),
                    reads=["gsc"], writes=["gsc"])
                pr.op("act", lambda e: e.activation(
                    gsc[:, 3:4], gsc[:, 2:3], AF.Exp, scale=-1.0),
                    reads=["gsc", "csts"], writes=["gsc"])
                pr.op("dve", lambda e: e.tensor_scalar(
                    gsc[:, 4:5], gsc[:, 3:4], 1.0, None, ALU.add),
                    reads=["gsc"], writes=["gsc"])
                pr.op("dve", lambda e: e.reciprocal(
                    gsc[:, 5:6], gsc[:, 4:5]), reads=["gsc"], writes=["gsc"])
                pr.op("dve", lambda e: e.tensor_scalar(
                    gsc[:, 6:7], gsc[:, 5:6], -1.0, 1.0, ALU.mult, ALU.add),
                    reads=["gsc"], writes=["gsc"])
                pr.op("dve", lambda e: e.tensor_scalar(
                    gsc[:, 24:32], gsc[:, 16:24], gsc[:, 1:2], None,
                    ALU.is_equal), reads=["gsc"], writes=["gsc"])
                pr.op("dve", lambda e: e.tensor_scalar(
                    gsc[:, 32:40], gsc[:, 8:16], gsc[:, 5:6], None, ALU.mult),
                    reads=["gsc"], writes=["gsc"])
                pr.op("dve", lambda e, i=i: e.scalar_tensor_tensor(
                    gts[:, i * 8:(i + 1) * 8], gsc[:, 24:32], gsc[:, 6:7],
                    gsc[:, 32:40], ALU.mult, ALU.add),
                    reads=["gsc"], writes=["gts"])
            for i in range(4):
                pr.op("pe", lambda e, i=i: e.transpose(
                    pF[0:8, 3584 + i * 128:3584 + (i + 1) * 128],
                    gts[:, i * 8:(i + 1) * 8], idf),
                    reads=["gts", "idf", "F7"], writes=["F7"])
            pr.op("dve", lambda e, t=t: e.tensor_copy(
                gfm[0:8, t * 512:(t + 1) * 512], pF[0:8, 3584:4096]),
                reads=["F7"], writes=["gfm"])
        for ti, (toff, W) in enumerate(FFW):
            for e_ in range(NE):
                pr.op("pe", lambda e, e_=e_, toff=toff, W=W: e.matmul(
                    pF[:, 3584:3584 + W], oneh[:, e_ * 128:(e_ + 1) * 128],
                    gfm[0:8, toff:toff + W], start=True, stop=True),
                    reads=["oneh", "gfm", "F7"], writes=["F7"])
                Gt = (P0 if e_ < 4 else P1)[:, (e_ % 4) * 512:(e_ % 4) * 512 + W]
                Gtn = ("P0" if e_ < 4 else "P1") + f".{e_ % 4}"
                pr.op("act", lambda e, W=W, Gt=Gt: e.activation(
                    Gt, pF[:, 3584:3584 + W], AF.Copy),
                    reads=["F7"], writes=[Gtn])
            for e_ in range(NE):
                ub = 4 * (e_ % 2)
                for k in range(4):
                    sv = sWeT[e_].rearrange("(dg p) j -> p dg j", p=128)
                    dvw = u[ub + k].rearrange("p (dg j) -> p dg j", j=1024)
                    dmaop("sp", lambda e, sv=sv, dvw=dvw, k=k: e.dma_start(
                        dvw, sv[:, 2 * k:2 * k + 2, :]),
                        f"u{ub + k}", reads=[f"sWeT{e_}"], writes=[f"u{ub + k}"])
                Gt = (P0 if e_ < 4 else P1)[:, (e_ % 4) * 512:(e_ % 4) * 512 + W]
                Gtn = ("P0" if e_ < 4 else "P1") + f".{e_ % 4}"
                for dc in range(DC):
                    rg, rgn = ring[dc % 4], f"ring{dc % 4}"
                    pr.op("dve", lambda e, dc=dc, toff=toff, W=W, rg=rg, Gt=Gt:
                          e.tensor_tensor(rg[:, 0:W],
                                          y[dc][:, toff:toff + W],
                                          Gt, ALU.mult),
                          reads=[f"y{dc}", Gtn], writes=[rgn])
                    for jc in range(DC):
                        r0 = jc * 512
                        pr.op("pe", lambda e, e_=e_, dc=dc, jc=jc, r0=r0,
                              W=W, rg=rg, ub=ub:
                              e.matmul(pF[:, r0:r0 + W],
                                       u[ub + dc // 2][:, (dc % 2) * 1024 +
                                                       jc * 128:(dc % 2) * 1024 +
                                                       (jc + 1) * 128],
                                       rg[:, 0:W],
                                       start=(e_ == 0 and dc == 0),
                                       stop=(e_ == NE - 1 and dc == DC - 1)),
                              reads=[f"u{ub + dc // 2}", rgn, f"F{jc}"],
                              writes=[f"F{jc}"])
            for jc in range(DC):
                r0 = jc * 512
                pr.op("dve", lambda e, jc=jc, toff=toff, W=W, r0=r0:
                      e.tensor_tensor(x[jc][:, toff:toff + W],
                                      x[jc][:, toff:toff + W],
                                      pF[:, r0:r0 + W], ALU.add),
                      reads=[f"x{jc}", f"F{jc}"], writes=[f"x{jc}"])

    gen_consts()
    if stage >= 2:
        prep_gate()
        prep_ffn()

    for b in range(NB):
        x_in(b)
        if stage >= 1:
            ln()
            attention("a")
        if stage >= 2:
            ln()
            ffn()
        if stage >= 3:
            ln(fp32_res=True)
            ln(src_is_y=True, affine=True)
            attention("b")
        if stage >= 4:
            ln()
            ffn()
        x_out(b)

    pr.finish()
    return nc


def _host_tables():
    half = DK // 2
    inv = 10000.0 ** (-np.arange(half, dtype=np.float64) * 2.0 / DK)
    ang = np.arange(SQ, dtype=np.float64)[None, :] * inv[:, None]
    cos = np.cos(ang).astype(np.float32)
    sin = np.sin(ang).astype(np.float32)
    T1 = np.zeros((128, SQ), np.float32)
    T2 = np.zeros((128, SQ), np.float32)
    for hb in (0, 64):
        T1[hb:hb + 32] = cos
        T1[hb + 32:hb + 64] = sin
        T2[hb:hb + 32] = sin
        T2[hb + 32:hb + 64] = cos
    Me = np.zeros((128, 128), np.float32)
    Mo = np.zeros((128, 128), np.float32)
    for hb in (0, 64):
        for j in range(32):
            Me[hb + j, hb + j] = 1.0
            Me[hb + 32 + j, hb + j] = -1.0
            Mo[hb + j, hb + 32 + j] = 1.0
            Mo[hb + 32 + j, hb + 32 + j] = 1.0
    return T1, T2, Me, Mo


def kernel(L0_attn_a_x, L0_attn_a_Wq, L0_attn_a_Wk, L0_attn_a_Wv, L0_attn_a_Wo,
           L0_attn_b_ln_w, L0_attn_b_ln_b, L0_attn_b_wq, L0_attn_b_wk,
           L0_attn_b_wv, L0_attn_b_wo, L0_ffn_w_gate, L0_ffn_w_in,
           L0_ffn_w_out):
    xfull = np.ascontiguousarray(np.asarray(L0_attn_a_x, dtype=np.float32))
    T1, T2, Me, Mo = _host_tables()
    perm = np.zeros(D, np.int64)
    for h in range(NH):
        for par in range(2):
            for j2 in range(32):
                perm[h * 64 + par * 32 + j2] = h * 64 + 2 * j2 + par
    common = {
        "aWqT": np.ascontiguousarray(
            np.asarray(L0_attn_a_Wq, dtype=np.float32)[perm].T),
        "aWkT": np.ascontiguousarray(
            np.asarray(L0_attn_a_Wk, dtype=np.float32)[perm].T),
        "aWvT": np.ascontiguousarray(
            np.asarray(L0_attn_a_Wv, dtype=np.float32).T),
        "aWoT": np.ascontiguousarray(
            np.asarray(L0_attn_a_Wo, dtype=np.float32).T),
        "blnw": np.ascontiguousarray(L0_attn_b_ln_w, dtype=np.float32),
        "blnb": np.ascontiguousarray(L0_attn_b_ln_b, dtype=np.float32),
        "bwq": np.ascontiguousarray(L0_attn_b_wq, dtype=np.float32),
        "bwk": np.ascontiguousarray(L0_attn_b_wk, dtype=np.float32),
        "bwv": np.ascontiguousarray(L0_attn_b_wv, dtype=np.float32),
        "bwo": np.ascontiguousarray(L0_attn_b_wo, dtype=np.float32),
        "fwg": np.ascontiguousarray(L0_ffn_w_gate, dtype=np.float32),
        "fwin": np.ascontiguousarray(L0_ffn_w_in, dtype=np.float32),
        "fwoutT": np.ascontiguousarray(
            np.asarray(L0_ffn_w_out, dtype=np.float32).transpose(0, 2, 1)),
        "tT1": T1, "tT2": T2, "tMe": Me, "tMo": Mo,
    }
    nc = build_nc(_STAGE)
    in_maps = []
    for c in range(NCORES):
        m = dict(common)
        m["x"] = xfull[c * NB:(c + 1) * NB]
        in_maps.append(m)
    res = run_bass_kernel_spmd(nc, in_maps, core_ids=list(range(NCORES)))
    return np.concatenate([r["out"] for r in res.results], axis=0)


# version 18
# speedup vs baseline: 1.1447x; 1.1447x over previous
import numpy as np
import ml_dtypes
import concourse.bass as bass
import concourse.mybir as mybir
from concourse.bass_utils import run_bass_kernel_spmd

F32 = mybir.dt.float32
BF16 = mybir.dt.bfloat16
AX = mybir.AxisListType.X
AF = mybir.ActivationFunctionType
ALU = mybir.AluOpType

D = 1024
SQ = 2048
NH = 16
DK = 64
NE = 8
HID = 4096
NCORES = 8
NB = 2
DC = D // 128
NKC = SQ // 128
EPS = 1e-5

_STAGE = 4


class Prog:
    SECTION = {"pe": "tensor", "act": "scalar", "dve": "vector",
               "pool": "gpsimd", "sp": "sync"}

    def __init__(self, nc):
        self.nc = nc
        names = ["pe", "act", "dve", "pool", "sp"]
        self.sems = {s: nc.alloc_semaphore("sem_" + s) for s in names}
        self.count = {s: 0 for s in names}
        self.items = {e: [] for e in Prog.SECTION}
        self.last_w = {}
        self.readers = {}
        self.waited = {e: {} for e in Prog.SECTION}

    def op(self, eng, fn, reads=(), writes=(), dma=False, dslot=None):
        if dma:
            assert dslot is not None
            sem, inc = "D_" + dslot, 16
            if sem not in self.sems:
                self.sems[sem] = self.nc.alloc_semaphore("sem_" + sem)
                self.count[sem] = 0
            reads = list(reads) + [f"@{dslot}"]
            writes = list(writes) + [f"@{dslot}"]
        else:
            sem, inc = eng, 1
        need = {}

        def add(s, v):
            if s == eng and not dma and eng == "pe":
                return
            if v > need.get(s, 0):
                need[s] = v

        for r in reads:
            lw = self.last_w.get(r)
            if lw:
                add(*lw)
        for r in writes:
            lw = self.last_w.get(r)
            if lw:
                add(*lw)
            for s, v in self.readers.get(r, {}).items():
                add(s, v)
        wt = self.waited[eng]
        waits = [(s, v) for s, v in need.items() if wt.get(s, 0) < v]
        for s, v in waits:
            wt[s] = v
        self.count[sem] += inc
        val = self.count[sem]
        self.items[eng].append((waits, fn, sem, inc))
        for r in writes:
            self.last_w[r] = (sem, val)
            self.readers[r] = {}
        for r in reads:
            d = self.readers.setdefault(r, {})
            if val > d.get(sem, 0):
                d[sem] = val

    def finish(self):
        nc = self.nc
        final = [(s, self.count[s]) for s in self.sems
                 if s.startswith("D_") and self.count[s] > 0]
        with nc.Block() as blk:
            for ename, sname in Prog.SECTION.items():
                items = self.items[ename]

                def make(ename, items):
                    def run(eng):
                        for waits, fn, sem, inc in items:
                            for s, v in waits:
                                eng.wait_ge(self.sems[s], v)
                            fn(eng).then_inc(self.sems[sem], inc)
                        if ename == "sp":
                            for s, v in final:
                                eng.wait_ge(self.sems[s], v)
                    return run

                getattr(blk, sname)(make(ename, items))


def _fb(c0, c1):
    return [f"F{i}" for i in range(c0 // 512, (c1 + 511) // 512)]


def build_nc(stage=4):
    nc = bass.Bass("TRN2", target_bir_lowering=False)
    pr = Prog(nc)

    dx = nc.dram_tensor("x", [NB, SQ, D], F32, kind="ExternalInput").ap()
    dWqT = nc.dram_tensor("aWqT", [D, D], F32, kind="ExternalInput").ap()
    dWkT = nc.dram_tensor("aWkT", [D, D], F32, kind="ExternalInput").ap()
    dWvT = nc.dram_tensor("aWvT", [D, D], F32, kind="ExternalInput").ap()
    dWoT = nc.dram_tensor("aWoT", [D, D], F32, kind="ExternalInput").ap()
    dlnw = nc.dram_tensor("blnw", [D], F32, kind="ExternalInput").ap()
    dlnb = nc.dram_tensor("blnb", [D], F32, kind="ExternalInput").ap()
    dbq = nc.dram_tensor("bwq", [NH, D, DK], F32, kind="ExternalInput").ap()
    dbk = nc.dram_tensor("bwk", [NH, D, DK], F32, kind="ExternalInput").ap()
    dbv = nc.dram_tensor("bwv", [NH, D, DK], F32, kind="ExternalInput").ap()
    dbo = nc.dram_tensor("bwo", [D, D], F32, kind="ExternalInput").ap()
    dgate = nc.dram_tensor("fwg", [NE, D], F32, kind="ExternalInput").ap()
    dwin = nc.dram_tensor("fwin", [NE, HID, D], F32, kind="ExternalInput").ap()
    dwoutT = nc.dram_tensor("fwoutT", [NE, HID, D], F32, kind="ExternalInput").ap()
    dT1 = nc.dram_tensor("tT1", [128, SQ], F32, kind="ExternalInput").ap()
    dT2 = nc.dram_tensor("tT2", [128, SQ], F32, kind="ExternalInput").ap()
    dMe = nc.dram_tensor("tMe", [128, 128], F32, kind="ExternalInput").ap()
    dMo = nc.dram_tensor("tMo", [128, 128], F32, kind="ExternalInput").ap()
    dout = nc.dram_tensor("out", [NB, SQ, D], F32, kind="ExternalOutput").ap()

    sWeT = nc.dram_tensor("sWeT", [NE, D, D], BF16).ap()

    A = nc.alloc_sbuf_tensor
    x = [A(f"x{c}", [128, SQ], F32).ap() for c in range(DC)]
    y = [A(f"y{c}", [128, SQ], BF16).ap() for c in range(DC)]
    u = [A(f"u{c}", [128, SQ], BF16).ap() for c in range(DC)]
    qk = [A(f"qk{i}", [128, SQ], BF16).ap() for i in range(3)]
    P0 = A("P0", [128, SQ], BF16).ap()
    P1 = A("P1", [128, SQ], BF16).ap()
    vtok = A("vtok", [128, NKC * 160], BF16).ap()
    ws = [A(f"ws{i}", [128, D], BF16).ap() for i in range(3)]
    wo2 = [A(f"wo2{i}", [128, D], BF16).ap() for i in range(2)]
    ropeT1 = A("ropeT1", [128, SQ], BF16).ap()
    ropeT2 = A("ropeT2", [128, SQ], BF16).ap()
    rMe = A("rMe", [128, 128], BF16).ap()
    rMo = A("rMo", [128, 128], BF16).ap()
    rowp = A("rowp", [66, SQ], F32).ap()
    rowsm = A("rowsm", [128, 16], F32).ap()
    bigtri = A("bigtri", [128, 640], BF16).ap()
    idf = A("idf", [128, 128], F32).ap()
    oneh = A("oneh", [8, NE * 128], BF16).ap()
    onescb = A("onescb", [128, 1], BF16).ap()
    ones1f = A("ones1f", [66, 128], F32).ap()
    wgT = A("wgT", [128, 64], BF16).ap()
    lnwb = A("lnwb", [128, 16], F32).ap()
    stag = A("stag", [128, 2048], F32).ap()
    tmpB = A("tmpB", [64, SQ], BF16).ap()
    gfm = A("gfm", [8, SQ], BF16).ap()
    lgt = A("lgt", [128, 32], F32).ap()
    gts = A("gts", [128, 32], F32).ap()
    lgs = A("lgs", [8, 512], F32).ap()
    gsc = A("gsc", [128, 48], F32).ap()
    Gs = A("Gs", [128, 512], BF16).ap()
    ring = [A(f"ring{i}", [128, 512], BF16).ap() for i in range(4)]
    _cvals = [1e-5, 1e-30, 0.5, -0.5, 0.125, -1.0]
    csts = A("csts", [128, len(_cvals)], F32).ap()
    for _i, _v in enumerate(_cvals):
        nc.const_aps.register(F32, _v, csts[:, _i:_i + 1])

    tmp = stag[:, 1024:1536]
    tmp2 = stag[:, 1536:2048]

    def yef_view(dc):
        t = P0 if dc < 4 else P1
        c0 = (dc % 4) * 512
        return t[:, c0:c0 + 512], ("P0" if dc < 4 else "P1")

    pF = nc.alloc_psum_tensor("pF", [128, 4096], F32).ap()

    def dmaop(q, fn, dslot, reads=(), writes=()):
        pr.op(q, fn, reads=reads, writes=writes, dma=True, dslot=dslot)

    def gen_consts():
        g = pr.op
        g("pool", lambda e: e.memset(idf, 0.0), writes=["idf"])
        g("pool", lambda e: e.affine_select(out=idf, in_=idf,
          compare_op=ALU.not_equal, fill=1.0, base=0, pattern=[[-1, 128]],
          channel_multiplier=1), reads=["idf"], writes=["idf"])
        g("pool", lambda e: e.memset(bigtri, 1.0), writes=["tri"])
        g("pool", lambda e: e.affine_select(out=bigtri, in_=bigtri,
          compare_op=ALU.is_ge, fill=0.0, base=-512, pattern=[[1, 640]],
          channel_multiplier=-1), reads=["tri"], writes=["tri"])
        g("pool", lambda e: e.memset(oneh, 0.0), writes=["oneh"])
        for e_ in range(NE):
            g("pool", lambda e, e_=e_: e.affine_select(
                out=oneh[:, e_ * 128:(e_ + 1) * 128],
                in_=oneh[:, e_ * 128:(e_ + 1) * 128],
                compare_op=ALU.not_equal, fill=1.0, base=-e_,
                pattern=[[0, 128]], channel_multiplier=1),
              reads=["oneh"], writes=["oneh"])
        g("pool", lambda e: e.memset(onescb, 1.0), writes=["onescb"])
        g("pool", lambda e: e.memset(ones1f, 1.0), writes=["ones1f"])
        g("pool", lambda e: e.memset(vtok, 1.0), writes=["vtok"])
        for _i, _v in enumerate(_cvals):
            g("pool", lambda e, _i=_i, _v=_v: e.memset(csts[:, _i:_i + 1], _v),
              writes=["csts"])
        dmaop("pool", lambda e: e.dma_start(ropeT1, dT1), "consts", writes=["ropeT1"])
        dmaop("pool", lambda e: e.dma_start(ropeT2, dT2), "consts", writes=["ropeT2"])
        dmaop("pool", lambda e: e.dma_start(rMe, dMe), "consts", writes=["rMe"])
        dmaop("pool", lambda e: e.dma_start(rMo, dMo), "consts", writes=["rMo"])
        lw = dlnw.rearrange("(c p one) -> c p one", p=128, one=1)
        lb = dlnb.rearrange("(c p one) -> c p one", p=128, one=1)
        for c in range(DC):
            dmaop("sp", lambda e, c=c: e.dma_start(lnwb[:, c:c + 1], lw[c]),
                  "lnwb", writes=["lnwb"])
            dmaop("sp", lambda e, c=c: e.dma_start(lnwb[:, 8 + c:9 + c], lb[c]),
                  "lnwb", writes=["lnwb"])

    def prep_gate():
        dmaop("sp", lambda e: e.dma_start(stag[0:8, 0:D], dgate), "lnwb",
              writes=["stag0", "stag1"])
        for c in range(DC):
            pr.op("pe", lambda e, c=c: e.transpose(
                pF[:, 3584:3592], stag[0:8, c * 128:(c + 1) * 128],
                idf[0:8, 0:8]),
                reads=["stag0", "stag1", "idf", "F7"], writes=["F7"])
            pr.op("dve", lambda e, c=c: e.tensor_copy(
                wgT[:, c * 8:(c + 1) * 8], pF[:, 3584:3592]),
                reads=["F7"], writes=["wgT"])

    def prep_ffn():
        dwv = dwin.rearrange("e (hg p) d -> e p hg d", p=128)
        dov = dwoutT.rearrange("e (hg p) j -> e p hg j", p=128)
        for e_ in range(NE):
            st_names = []
            for jb in range(2):
                for dcg in range(2):
                    for hg in range(8):
                        wbl = [qk[0], qk[1], P1, u[0]]
                        wbns = ["pqk0", "pqk1", "pP1x", "pu0"]
                        wb, wbn = wbl[hg % 4], wbns[hg % 4]
                        obl = [qk[2], P0, u[1], u[2]]
                        obnl = ["pqk2", "pP0x", "pu1", "pu2"]
                        ob, obn = obl[hg % 4], obnl[hg % 4]
                        obns = [obn]
                        dmaop("pool",
                              lambda e, e_=e_, hg=hg, dcg=dcg, wb=wb:
                              e.dma_start(
                                  wb.rearrange("p (hc c) -> p hc c", hc=4),
                                  dwv[e_][:, hg * 4:(hg + 1) * 4,
                                          dcg * 512:(dcg + 1) * 512]),
                              wbn, reads=[], writes=[wbn])
                        dmaop("pool",
                              lambda e, e_=e_, hg=hg, jb=jb, ob=ob:
                              e.dma_start(
                                  ob.rearrange("p (hc c) -> p hc c", hc=4),
                                  dov[e_][:, hg * 4:(hg + 1) * 4,
                                          jb * 512:(jb + 1) * 512]),
                              obn, reads=[], writes=obns)
                        for hc4 in range(4):
                            for dci in range(4):
                                pr.op("pe", lambda e, wb=wb, ob=ob, dci=dci,
                                      hg=hg, hc4=hc4:
                                      e.matmul(pF[:, dci * 512:(dci + 1) * 512],
                                               wb[:, hc4 * 512 + dci * 128:
                                                  hc4 * 512 + (dci + 1) * 128],
                                               ob[:, hc4 * 512:(hc4 + 1) * 512],
                                               start=(hg == 0 and hc4 == 0),
                                               stop=(hg == 7 and hc4 == 3)),
                                      reads=[wbn] + obns, writes=[f"F{dci}"])
                    for dci in range(4):
                        dc_ = dcg * 4 + dci
                        sb, sbn = wo2[dci % 2], f"wo2{dci % 2}"
                        pr.op("dve", lambda e, dci=dci, sb=sb: e.tensor_copy(
                            sb[:, 0:512], pF[:, dci * 512:(dci + 1) * 512]),
                            reads=[f"F{dci}"], writes=[sbn])
                        nm = f"sWeT{e_}.{jb}.{dc_}"
                        st_names.append(nm)
                        dmaop("sp", lambda e, e_=e_, dc_=dc_, jb=jb, sb=sb:
                              e.dma_start(sWeT[e_][dc_ * 128:(dc_ + 1) * 128,
                                                   jb * 512:(jb + 1) * 512],
                                          sb[:, 0:512]),
                              f"stw{dci % 2}", reads=[sbn], writes=[nm])
            pr.op("pool", lambda e: e.nop(),
                  reads=st_names, writes=[f"sWeT{e_}"])

    def x_in(b):
        for t in range(SQ // 128):
            sg = stag[:, (t % 2) * 1024:(t % 2) * 1024 + 1024]
            sgn = ["stag0"] if t % 2 == 0 else ["stag1", "stag2"]
            dmaop("sp", lambda e, b=b, t=t, sg=sg: e.dma_start(
                sg, dx[b, t * 128:(t + 1) * 128, :]), f"stg{t % 2}",
                writes=sgn)
            for c in range(DC):
                pc = pF[:, 2048 + (c % 4) * 256:2048 + (c % 4) * 256 + 128]
                bank = f"F{4 + (c % 4) // 2}"
                pr.op("pe", lambda e, c=c, sg=sg, pc=pc: e.transpose(
                    pc, sg[:, c * 128:(c + 1) * 128], idf),
                    reads=sgn + ["idf", bank], writes=[bank])
                pr.op("dve", lambda e, c=c, t=t, pc=pc: e.tensor_copy(
                    x[c][:, t * 128:(t + 1) * 128], pc),
                    reads=[bank], writes=[f"x{c}"])

    def x_out(b):
        for t in range(SQ // 128):
            sg = stag[:, (t % 2) * 1024:(t % 2) * 1024 + 1024]
            sgn = ["stag0"] if t % 2 == 0 else ["stag1", "stag2"]
            for c in range(DC):
                pc = pF[:, 2048 + (c % 4) * 256:2048 + (c % 4) * 256 + 128]
                bank = f"F{4 + (c % 4) // 2}"
                pr.op("pe", lambda e, c=c, t=t, pc=pc: e.transpose(
                    pc, x[c][:, t * 128:(t + 1) * 128], idf),
                    reads=[f"x{c}", "idf", bank], writes=[bank])
                pr.op("dve", lambda e, c=c, sg=sg, pc=pc: e.tensor_copy(
                    sg[:, c * 128:(c + 1) * 128], pc),
                    reads=[bank], writes=sgn)
            dmaop("sp", lambda e, b=b, t=t, sg=sg: e.dma_start(
                dout[b, t * 128:(t + 1) * 128, :], sg),
                f"stg{t % 2}", reads=sgn, writes=[f"dout{b}.{t}"])

    def ln(src_is_y=False, fp32_res=False, affine=False):
        src = y if src_is_y else x
        sn = "y" if src_is_y else "x"
        s1v = pF[0:1, 2048:2560]
        s2v = pF[0:1, 2560:3072]
        for t in range(4):
            for c in range(DC):
                if src_is_y:
                    xb, xbn = src[c][:, t * 512:(t + 1) * 512], f"y{c}"
                else:
                    xb, xbn = ring[c % 2], f"ring{c % 2}"
                    pr.op("pool", lambda e, c=c, t=t, xb=xb: e.tensor_copy(
                        xb, x[c][:, t * 512:(t + 1) * 512]),
                        reads=[f"x{c}"], writes=[xbn])
                sq, sqn = ring[2 + c % 2], f"ring{2 + c % 2}"
                pr.op("act", lambda e, c=c, t=t, sq=sq: e.activation(
                    sq, src[c][:, t * 512:(t + 1) * 512], AF.Square),
                    reads=[f"{sn}{c}"], writes=[sqn])
                pr.op("pe", lambda e, c=c, xb=xb: e.matmul(
                    s1v, onescb, xb, start=(c == 0), stop=(c == DC - 1)),
                    reads=[xbn, "onescb", "F4"], writes=["F4"])
                pr.op("pe", lambda e, c=c, sq=sq: e.matmul(
                    s2v, onescb, sq, start=(c == 0), stop=(c == DC - 1)),
                    reads=[sqn, "onescb", "F5"], writes=["F5"])
            pr.op("dve", lambda e, t=t: e.tensor_copy(
                rowp[0:1, t * 512:(t + 1) * 512], s1v),
                reads=["F4"], writes=["row0"])
            pr.op("dve", lambda e, t=t: e.tensor_copy(
                rowp[32:33, t * 512:(t + 1) * 512], s2v),
                reads=["F5"], writes=["row32"])
        pr.op("dve", lambda e: e.tensor_scalar(
            rowp[0:1, :], rowp[0:1, :], -1.0 / D, None, ALU.mult),
            reads=["row0"], writes=["row0"])
        for t in range(4):
            pr.op("dve", lambda e, t=t: e.tensor_tensor(
                pF[0:1, 3072:3584], rowp[0:1, t * 512:(t + 1) * 512],
                rowp[0:1, t * 512:(t + 1) * 512], ALU.mult),
                reads=["row0", "F6"], writes=["F6"])
            pr.op("dve", lambda e, t=t: e.scalar_tensor_tensor(
                rowp[32:33, t * 512:(t + 1) * 512],
                rowp[32:33, t * 512:(t + 1) * 512], 1.0 / D,
                pF[0:1, 3072:3584], ALU.mult, ALU.subtract),
                reads=["row32", "F6"], writes=["row32"])
        pr.op("act", lambda e: e.activation(
            rowp[32:33, :], rowp[32:33, :], AF.Ln, bias=EPS),
            reads=["row32", "csts"], writes=["row32"])
        pr.op("act", lambda e: e.activation(
            rowp[32:33, :], rowp[32:33, :], AF.Exp, scale=-0.5),
            reads=["row32", "csts"], writes=["row32"])
        for t in range(4):
            pr.op("pe", lambda e, t=t: e.matmul(
                pF[:, 2048:2560], ones1f[0:1, :], rowp[0:1, t * 512:(t + 1) * 512],
                start=True, stop=True),
                reads=["row0", "ones1f", "F4"], writes=["F4"])
            pr.op("pe", lambda e, t=t: e.matmul(
                pF[:, 2560:3072], ones1f[32:33, :], rowp[32:33, t * 512:(t + 1) * 512],
                start=True, stop=True),
                reads=["row32", "ones1f", "F5"], writes=["F5"])
            for c in range(DC):
                sl = (t * 512, (t + 1) * 512)
                pr.op("dve", lambda e, c=c, sl=sl: e.tensor_tensor(
                    tmp, src[c][:, sl[0]:sl[1]], pF[:, 2048:2560], ALU.add),
                    reads=[f"{sn}{c}", "F4"], writes=["stag1"])
                if affine:
                    pr.op("dve", lambda e, c=c, sl=sl: e.tensor_tensor(
                        tmp2, tmp, pF[:, 2560:3072], ALU.mult),
                        reads=["stag1", "F5"], writes=["stag2"])
                    pr.op("dve", lambda e, c=c, sl=sl: e.tensor_scalar(
                        y[c][:, sl[0]:sl[1]], tmp2,
                        lnwb[:, c:c + 1], lnwb[:, 8 + c:9 + c],
                        ALU.mult, ALU.add),
                        reads=["stag2", "lnwb"], writes=[f"y{c}"])
                else:
                    pr.op("dve", lambda e, c=c, sl=sl: e.tensor_tensor(
                        y[c][:, sl[0]:sl[1]], tmp, pF[:, 2560:3072], ALU.mult),
                        reads=["stag1", "F5"], writes=[f"y{c}"])
                    if fp32_res:
                        pr.op("dve", lambda e, c=c, sl=sl: e.tensor_tensor(
                            tmp2, tmp, pF[:, 2560:3072], ALU.mult),
                            reads=["stag1", "F5"], writes=["stag2"])
                        pr.op("dve", lambda e, c=c, sl=sl: e.tensor_tensor(
                            x[c][:, sl[0]:sl[1]], x[c][:, sl[0]:sl[1]],
                            tmp2, ALU.add),
                            reads=[f"x{c}", "stag2"], writes=[f"x{c}"])

    def attention(layer):
        for hp in range(NH // 2):
            if hp % 2 == 0:
                wtile = [(lambda dc, ti=ti: ws[ti][:, dc * 128:(dc + 1) * 128])
                         for ti in range(3)]
                wdst = [ws[0], ws[1], ws[2]]
                wtok = ["ws0", "ws1", "ws2"]
            else:
                if hp < 7:
                    ualt = u[hp + 1]
                    vtok_n = f"u{hp + 1}"
                else:
                    ualt = ws[2]
                    vtok_n = "ws2"
                wtile = [
                    lambda dc: (ring[0] if dc < 4 else ring[1])[
                        :, (dc % 4) * 128:(dc % 4 + 1) * 128],
                    lambda dc: (ring[2] if dc < 4 else ring[3])[
                        :, (dc % 4) * 128:(dc % 4 + 1) * 128],
                    lambda dc, ualt=ualt: ualt[:, dc * 128:(dc + 1) * 128],
                ]
                wdst = [None, None, ualt[:, 0:1024]]
                wtok = [None, None, vtok_n]
            if layer == "a":
                for ti, src in enumerate((dWqT, dWkT, dWvT)):
                    sv = src.rearrange("(dc p) j -> p dc j", p=128)
                    if hp % 2 == 0:
                        dvw = ws[ti].rearrange("p (dc j) -> p dc j", j=128)
                        dmaop("pool", lambda e, sv=sv, dvw=dvw, hp=hp:
                              e.dma_start(dvw, sv[:, :, hp * 128:(hp + 1) * 128]),
                              f"ws{ti}", reads=[], writes=[f"ws{ti}"])
                    elif ti < 2:
                        for half in range(2):
                            rg = ring[ti * 2 + half]
                            rtok = f"ring{ti * 2 + half}"
                            dvw = rg.rearrange("p (dc j) -> p dc j", j=128)
                            dmaop("pool", lambda e, sv=sv, dvw=dvw, hp=hp,
                                  half=half:
                                  e.dma_start(dvw,
                                              sv[:, half * 4:(half + 1) * 4,
                                                 hp * 128:(hp + 1) * 128]),
                                  rtok, reads=[], writes=[rtok])
                    else:
                        dvw = wdst[2].rearrange("p (dc j) -> p dc j", j=128)
                        dmaop("pool", lambda e, sv=sv, dvw=dvw, hp=hp:
                              e.dma_start(dvw, sv[:, :, hp * 128:(hp + 1) * 128]),
                              "wsv2", reads=[], writes=[wtok[2]])
            else:
                for ti, src in ((0, dbq), (1, dbk), (2, dbv)):
                    sv8 = src.rearrange("h (dc p) e -> h p dc e", p=128)
                    for h2 in range(2):
                        h = 2 * hp + h2
                        if hp % 2 == 0:
                            dvw = ws[ti].rearrange("p (dc he) -> p dc he",
                                                   he=128)
                            dmaop("pool", lambda e, sv8=sv8, dvw=dvw, h=h,
                                  h2=h2:
                                  e.dma_start(dvw[:, :, h2 * 64:(h2 + 1) * 64],
                                              sv8[h]),
                                  f"ws{ti}", reads=[], writes=[f"ws{ti}"])
                        elif ti < 2:
                            for half in range(2):
                                rg = ring[ti * 2 + half]
                                rtok = f"ring{ti * 2 + half}"
                                dvw = rg.rearrange("p (dc he) -> p dc he",
                                                   he=128)
                                dmaop("pool", lambda e, sv8=sv8, dvw=dvw,
                                      h=h, h2=h2, half=half:
                                      e.dma_start(
                                          dvw[:, :, h2 * 64:(h2 + 1) * 64],
                                          sv8[h][:, half * 4:(half + 1) * 4]),
                                      rtok, reads=[], writes=[rtok])
                        else:
                            dvw = wdst[2].rearrange("p (dc he) -> p dc he",
                                                    he=128)
                            dmaop("pool", lambda e, sv8=sv8, dvw=dvw, h=h,
                                  h2=h2:
                                  e.dma_start(dvw[:, :, h2 * 64:(h2 + 1) * 64],
                                              sv8[h]),
                                  "wsv2", reads=[], writes=[wtok[2]])
            if hp % 2 == 0:
                wrd = [["ws0"], ["ws1"], ["ws2"]]
            else:
                wrd = [["ring0", "ring1"], ["ring2", "ring3"], [wtok[2]]]
            for ti in range(3):
                po = (ti % 2) * 2048
                for dc in range(DC):
                    lh = wtile[ti](dc)
                    for t in range(4):
                        pr.op("pe", lambda e, lh=lh, dc=dc, t=t, po=po:
                              e.matmul(
                                  pF[:, po + t * 512:po + (t + 1) * 512],
                                  lh,
                                  y[dc][:, t * 512:(t + 1) * 512],
                                  start=(dc == 0), stop=(dc == DC - 1)),
                              reads=wrd[ti] + [f"y{dc}", f"F{po // 512 + t}"],
                              writes=[f"F{po // 512 + t}"])
                for t in range(4):
                    pr.op("dve", lambda e, ti=ti, t=t, po=po: e.tensor_copy(
                        qk[ti][:, t * 512:(t + 1) * 512],
                        pF[:, po + t * 512:po + (t + 1) * 512]),
                        reads=[f"F{po // 512 + t}"], writes=[f"qk{ti}"])
            if layer == "a":
                for ti in range(2):
                    po = ((ti + 1) % 2) * 2048
                    pr.op("dve", lambda e, ti=ti: e.tensor_tensor(
                        P0, qk[ti], ropeT1, ALU.mult),
                        reads=[f"qk{ti}", "ropeT1"],
                        writes=[f"P0.{i}" for i in range(4)])
                    pr.op("dve", lambda e, ti=ti: e.tensor_tensor(
                        P1, qk[ti], ropeT2, ALU.mult),
                        reads=[f"qk{ti}", "ropeT2"],
                        writes=[f"P1.{i}" for i in range(4)])
                    for t in range(4):
                        pr.op("pe", lambda e, t=t, po=po: e.matmul(
                            pF[:, po + t * 512:po + (t + 1) * 512], rMe,
                            P0[:, t * 512:(t + 1) * 512],
                            start=True, stop=False),
                            reads=["rMe", f"P0.{t}", f"F{po // 512 + t}"],
                            writes=[f"F{po // 512 + t}"])
                        pr.op("pe", lambda e, t=t, po=po: e.matmul(
                            pF[:, po + t * 512:po + (t + 1) * 512], rMo,
                            P1[:, t * 512:(t + 1) * 512],
                            start=False, stop=True),
                            reads=["rMo", f"P1.{t}", f"F{po // 512 + t}"],
                            writes=[f"F{po // 512 + t}"])
                    for t in range(4):
                        pr.op("dve", lambda e, ti=ti, t=t, po=po: e.tensor_copy(
                            qk[ti][:, t * 512:(t + 1) * 512],
                            pF[:, po + t * 512:po + (t + 1) * 512]),
                            reads=[f"F{po // 512 + t}"], writes=[f"qk{ti}"])
            for kc in range(NKC):
                dmaop("sp", lambda e, kc=kc: e.dma_start(
                    vtok[:, kc * 160:kc * 160 + 64],
                    qk[2][0:64, kc * 128:(kc + 1) * 128], transpose=True),
                    f"vtA{kc % 2}", reads=["qk2"], writes=["vtok"])
                dmaop("act", lambda e, kc=kc: e.dma_start(
                    vtok[:, kc * 160 + 80:kc * 160 + 144],
                    qk[2][64:128, kc * 128:(kc + 1) * 128], transpose=True),
                    f"vtB{kc % 2}", reads=["qk2"], writes=["vtok"])
            for ti in (1, 0):
                for h2 in range(2):
                    hb = 64 * h2
                    rowr = rowp[0:1, :] if h2 == 0 else rowp[32:33, :]
                    rown = "row0" if h2 == 0 else "row32"
                    pr.op("act", lambda e, ti=ti, hb=hb, hp=hp: e.activation(
                        u[hp][hb:hb + 64, 0:SQ], qk[ti][hb:hb + 64, :],
                        AF.Square), reads=[f"qk{ti}"], writes=[f"u{hp}"])
                    for t in range(4):
                        po = 3072 + ((t + h2) % 2) * 512
                        pb = f"F{po // 512}"
                        pr.op("pe", lambda e, t=t, hb=hb, hp=hp, po=po:
                              e.matmul(pF[0:1, po:po + 512],
                                       onescb[hb:hb + 64, :],
                                       u[hp][hb:hb + 64, t * 512:(t + 1) * 512],
                                       start=True, stop=True),
                              reads=[f"u{hp}", "onescb", pb], writes=[pb])
                        pr.op("dve", lambda e, t=t, rowr=rowr, po=po:
                              e.tensor_copy(rowr[:, t * 512:(t + 1) * 512],
                                            pF[0:1, po:po + 512]),
                              reads=[pb], writes=[rown])
                    col = h2 * 8 + (0 if ti == 1 else 1)
                    pr.op("dve", lambda e, col=col, rowr=rowr: e.reduce_max(
                        rowsm[0:1, col:col + 1], rowr, axis=AX),
                        reads=[rown], writes=["rowsm"])
            for h2 in range(2):
                cb = h2 * 8
                pr.op("dve", lambda e, cb=cb: e.tensor_tensor(
                    rowsm[0:1, cb + 2:cb + 3], rowsm[0:1, cb:cb + 1],
                    rowsm[0:1, cb + 1:cb + 2], ALU.mult),
                    reads=["rowsm"], writes=["rowsm"])
                pr.op("act", lambda e, cb=cb: e.activation(
                    rowsm[0:1, cb + 3:cb + 4], rowsm[0:1, cb + 2:cb + 3],
                    AF.Ln, bias=1e-30),
                    reads=["rowsm", "csts"], writes=["rowsm"])
                pr.op("act", lambda e, cb=cb: e.activation(
                    rowsm[0:1, cb + 3:cb + 4], rowsm[0:1, cb + 3:cb + 4],
                    AF.Exp, scale=0.5),
                    reads=["rowsm", "csts"], writes=["rowsm"])
                pr.op("dve", lambda e, cb=cb: e.tensor_scalar(
                    rowsm[0:1, cb + 4:cb + 5], rowsm[0:1, cb + 3:cb + 4],
                    -1.02 / 8.0, -1.0, ALU.mult, ALU.add),
                    reads=["rowsm"], writes=["rowsm"])
                po = 3072 + h2 * 512
                pb = f"F{po // 512}"
                pr.op("pe", lambda e, cb=cb, po=po: e.matmul(
                    pF[:, po:po + 1], ones1f[0:1, :], rowsm[0:1, cb + 4:cb + 5],
                    start=True, stop=True),
                    reads=["rowsm", "ones1f", pb], writes=[pb])
                pr.op("dve", lambda e, cb=cb, po=po: e.tensor_copy(
                    rowsm[:, cb + 5:cb + 6], pF[:, po:po + 1]),
                    reads=[pb], writes=["rowsm"])
            for qh in range(2):
                nkc = 8 if qh == 0 else NKC

                def emit_pv(h2, kc, qh=qh):
                    qbl0 = max(0, kc // 4 - 2 * qh)
                    P = P0 if h2 == 0 else P1
                    for qbl in range(qbl0, 2):
                        qb = 2 * qh + qbl
                        Pn = (("P0" if h2 == 0 else "P1") +
                              f".{2 * (kc % 2) + qbl}")
                        reg = (h2 * 2 + qbl) * 512
                        pr.op("pe", lambda e, kc=kc, qb=qb, h2=h2, P=P,
                              reg=reg, qbl=qbl, qh=qh:
                              e.matmul(pF[0:65, reg:reg + 512],
                                       vtok[:, kc * 160 + h2 * 80:
                                            kc * 160 + h2 * 80 + 65],
                                       P[:, (kc % 2) * 1024 + qbl * 512:
                                         (kc % 2) * 1024 + (qbl + 1) * 512],
                                       start=(kc == 0),
                                       stop=(kc == min(4 * qb + 3, nkc - 1))),
                              reads=["vtok", Pn, f"F{reg // 512}"],
                              writes=[f"F{reg // 512}"])

                pend = [None, None]
                for kc in range(nkc):
                    for h2 in range(2):
                        hb = 64 * h2
                        bias_ap = rowsm[:, h2 * 8 + 5:h2 * 8 + 6]
                        qbl0 = max(0, kc // 4 - 2 * qh)
                        P = P0 if h2 == 0 else P1
                        PN = "P0" if h2 == 0 else "P1"
                        stbase = 2048 + h2 * 1024
                        for qbl in range(qbl0, 2):
                            qb = 2 * qh + qbl
                            stoff = stbase + qbl * 512
                            stb = f"F{stoff // 512}"
                            pr.op("pe", lambda e, kc=kc, qb=qb, hb=hb,
                                  stoff=stoff:
                                  e.matmul(pF[:, stoff:stoff + 512],
                                           qk[1][hb:hb + 64,
                                                 kc * 128:(kc + 1) * 128],
                                           qk[0][hb:hb + 64,
                                                 qb * 512:(qb + 1) * 512],
                                           start=True, stop=True),
                                  reads=["qk1", "qk0", stb], writes=[stb])
                        w_ = 512 * (2 - qbl0)
                        stoff = stbase + qbl0 * 512
                        banks = [f"F{(stoff // 512) + i}" for i in range(2 - qbl0)]
                        pr.op("act", lambda e, P=P, kc=kc, qbl0=qbl0, w_=w_,
                              stoff=stoff, bias_ap=bias_ap:
                              e.activation(
                                  P[:, (kc % 2) * 1024 + qbl0 * 512:
                                    (kc % 2) * 1024 + qbl0 * 512 + w_],
                                  pF[:, stoff:stoff + w_],
                                  AF.Exp, scale=0.125, bias=bias_ap),
                              reads=banks + ["csts", "rowsm"],
                              writes=[f"{PN}.{2 * (kc % 2) + qbl0 + i}"
                                      for i in range(2 - qbl0)])
                        qbl_d = kc // 4 - 2 * qh
                        if 0 <= qbl_d <= 1:
                            o_ = 128 * (kc % 4)
                            mtok = f"{PN}.{2 * (kc % 2) + qbl_d}"
                            pr.op("dve", lambda e, P=P, kc=kc, qbl_d=qbl_d,
                                  o_=o_:
                                  e.tensor_tensor(
                                      P[:, (kc % 2) * 1024 + qbl_d * 512:
                                        (kc % 2) * 1024 + qbl_d * 512 + o_ + 128],
                                      P[:, (kc % 2) * 1024 + qbl_d * 512:
                                        (kc % 2) * 1024 + qbl_d * 512 + o_ + 128],
                                      bigtri[:, 512 - o_:640], ALU.mult),
                                  reads=[mtok, "tri"], writes=[mtok])
                    for h2 in range(2):
                        if pend[h2] is not None:
                            emit_pv(h2, pend[h2])
                        pend[h2] = kc
                for h2 in range(2):
                    emit_pv(h2, pend[h2])
                for h2 in range(2):
                    lr = (rowp[64:65, :] if h2 == 0 else rowp[32:33, :])
                    lr = lr[:, qh * 1024:(qh + 1) * 1024]
                    lrn = ("row64" if h2 == 0 else "row32")
                    lones = (ones1f[64:65, 0:64] if h2 == 0
                             else ones1f[32:33, 0:64])
                    lrow_psum = pF[64:65, h2 * 1024:h2 * 1024 + 1024]
                    fb2 = [f"F{h2 * 2}", f"F{h2 * 2 + 1}"]
                    pr.op("dve", lambda e, lr=lr, lrow_psum=lrow_psum:
                          e.tensor_copy(lr, lrow_psum),
                          reads=fb2, writes=[lrn])
                    pr.op("act", lambda e, lr=lr: e.activation(
                        lr, lr, AF.Ln, bias=1e-30),
                        reads=[lrn, "csts"], writes=[lrn])
                    pr.op("act", lambda e, lr=lr: e.activation(
                        lr, lr, AF.Exp, scale=-1.0),
                        reads=[lrn, "csts"], writes=[lrn])
                    bco = 2048 + h2 * 1024
                    bcb = f"F{bco // 512}"
                    for t2 in range(2):
                        tg = qh * 1024 + t2 * 512
                        pr.op("pe", lambda e, t2=t2, lr=lr, lones=lones,
                              bco=bco: e.matmul(
                                  pF[0:64, bco:bco + 512], lones,
                                  lr[:, t2 * 512:(t2 + 1) * 512],
                                  start=True, stop=True),
                              reads=[lrn, "ones1f", bcb], writes=[bcb])
                        reg = h2 * 1024 + t2 * 512
                        regb = f"F{reg // 512}"
                        if h2 == 0:
                            pr.op("dve", lambda e, tg=tg, hp=hp, reg=reg:
                                  e.tensor_copy(
                                      u[hp][0:64, tg:tg + 512],
                                      pF[0:64, reg:reg + 512]),
                                  reads=[regb], writes=[f"u{hp}"])
                            pr.op("dve", lambda e, tg=tg, hp=hp, bco=bco:
                                  e.tensor_tensor(
                                      u[hp][0:64, tg:tg + 512],
                                      u[hp][0:64, tg:tg + 512],
                                      pF[0:64, bco:bco + 512], ALU.mult),
                                  reads=[f"u{hp}", bcb], writes=[f"u{hp}"])
                        else:
                            pr.op("dve", lambda e, tg=tg, reg=reg:
                                  e.tensor_copy(
                                      tmpB[0:64, tg:tg + 512],
                                      pF[0:64, reg:reg + 512]),
                                  reads=[regb], writes=["tmpB"])
                            pr.op("dve", lambda e, tg=tg, bco=bco:
                                  e.tensor_tensor(
                                      tmpB[0:64, tg:tg + 512],
                                      tmpB[0:64, tg:tg + 512],
                                      pF[0:64, bco:bco + 512], ALU.mult),
                                  reads=["tmpB", bcb], writes=["tmpB"])
                            pr.op("dve", lambda e, tg=tg, hp=hp:
                                  e.tensor_copy(
                                      u[hp][64:128, tg:tg + 512],
                                      tmpB[0:64, tg:tg + 512]),
                                  reads=["tmpB"], writes=[f"u{hp}"])
        srcT = dWoT if layer == "a" else dbo
        for jc in range(DC):
            buf, bn = wo2[jc % 2], f"wo2{jc % 2}"
            sv = srcT.rearrange("(dc p) j -> p dc j", p=128)
            dvw = buf.rearrange("p (dc j) -> p dc j", j=128)
            dmaop("pool", lambda e, sv=sv, dvw=dvw, jc=jc: e.dma_start(
                dvw, sv[:, :, jc * 128:(jc + 1) * 128]),
                f"wol{jc % 2}", reads=[], writes=[bn])
            for dc in range(DC):
                for t in range(4):
                    pr.op("pe", lambda e, buf=buf, dc=dc, t=t: e.matmul(
                        pF[:, t * 512:(t + 1) * 512],
                        buf[:, dc * 128:(dc + 1) * 128],
                        u[dc][:, t * 512:(t + 1) * 512],
                        start=(dc == 0), stop=(dc == DC - 1)),
                        reads=[bn, f"u{dc}", f"F{t}"], writes=[f"F{t}"])
            for t in range(4):
                pr.op("dve", lambda e, jc=jc, t=t: e.tensor_tensor(
                    x[jc][:, t * 512:(t + 1) * 512],
                    x[jc][:, t * 512:(t + 1) * 512],
                    pF[:, t * 512:(t + 1) * 512], ALU.add),
                    reads=[f"x{jc}", f"F{t}"], writes=[f"x{jc}"])

    FFW = [(0, 512), (512, 512), (1024, 512), (1536, 512)]

    def ffn():
        for t in range(4):
            for dc in range(DC):
                pr.op("pe", lambda e, dc=dc, t=t: e.matmul(
                    pF[0:8, 3072:3584], wgT[:, dc * 8:(dc + 1) * 8],
                    y[dc][:, t * 512:(t + 1) * 512],
                    start=(dc == 0), stop=(dc == DC - 1)),
                    reads=[f"y{dc}", "wgT", "F6"], writes=["F6"])
            pr.op("act", lambda e, t=t: e.activation(
                lgs, pF[0:8, 3072:3584], AF.Copy),
                reads=["F6"], writes=["lgs"])
            for i in range(4):
                pr.op("pe", lambda e, i=i: e.transpose(
                    pF[:, 3584 + i * 8:3592 + i * 8],
                    lgs[0:8, i * 128:(i + 1) * 128], idf[0:8, 0:8]),
                    reads=["lgs", "idf", "F7"], writes=["F7"])
            pr.op("dve", lambda e, t=t: e.tensor_copy(
                lgt[:, 0:32], pF[:, 3584:3616]),
                reads=["F7"], writes=["lgt"])
            for i in range(4):
                lv = lgt[:, i * 8:(i + 1) * 8]
                pr.op("dve", lambda e, lv=lv: e.reduce_max(
                    gsc[:, 0:1], lv, axis=AX), reads=["lgt"], writes=["gsc"])
                pr.op("dve", lambda e, lv=lv: e.tensor_scalar(
                    gsc[:, 8:16], lv, gsc[:, 0:1], None, ALU.is_equal),
                    reads=["lgt", "gsc"], writes=["gsc"])
                pr.op("dve", lambda e, lv=lv: e.scalar_tensor_tensor(
                    gsc[:, 16:24], gsc[:, 8:16], -1e30, lv,
                    ALU.mult, ALU.add), reads=["gsc", "lgt"], writes=["gsc"])
                pr.op("dve", lambda e: e.reduce_max(
                    gsc[:, 1:2], gsc[:, 16:24], axis=AX),
                    reads=["gsc"], writes=["gsc"])
                pr.op("dve", lambda e: e.tensor_tensor(
                    gsc[:, 2:3], gsc[:, 0:1], gsc[:, 1:2], ALU.subtract),
                    reads=["gsc"], writes=["gsc"])
                pr.op("act", lambda e: e.activation(
                    gsc[:, 3:4], gsc[:, 2:3], AF.Exp, scale=-1.0),
                    reads=["gsc", "csts"], writes=["gsc"])
                pr.op("dve", lambda e: e.tensor_scalar(
                    gsc[:, 4:5], gsc[:, 3:4], 1.0, None, ALU.add),
                    reads=["gsc"], writes=["gsc"])
                pr.op("dve", lambda e: e.reciprocal(
                    gsc[:, 5:6], gsc[:, 4:5]), reads=["gsc"], writes=["gsc"])
                pr.op("dve", lambda e: e.tensor_scalar(
                    gsc[:, 6:7], gsc[:, 5:6], -1.0, 1.0, ALU.mult, ALU.add),
                    reads=["gsc"], writes=["gsc"])
                pr.op("dve", lambda e: e.tensor_scalar(
                    gsc[:, 24:32], gsc[:, 16:24], gsc[:, 1:2], None,
                    ALU.is_equal), reads=["gsc"], writes=["gsc"])
                pr.op("dve", lambda e: e.tensor_scalar(
                    gsc[:, 32:40], gsc[:, 8:16], gsc[:, 5:6], None, ALU.mult),
                    reads=["gsc"], writes=["gsc"])
                pr.op("dve", lambda e, i=i: e.scalar_tensor_tensor(
                    gts[:, i * 8:(i + 1) * 8], gsc[:, 24:32], gsc[:, 6:7],
                    gsc[:, 32:40], ALU.mult, ALU.add),
                    reads=["gsc"], writes=["gts"])
            for i in range(4):
                pr.op("pe", lambda e, i=i: e.transpose(
                    pF[0:8, 3584 + i * 128:3584 + (i + 1) * 128],
                    gts[:, i * 8:(i + 1) * 8], idf),
                    reads=["gts", "idf", "F7"], writes=["F7"])
            pr.op("dve", lambda e, t=t: e.tensor_copy(
                gfm[0:8, t * 512:(t + 1) * 512], pF[0:8, 3584:4096]),
                reads=["F7"], writes=["gfm"])
        for ti, (toff, W) in enumerate(FFW):
            for e_ in range(NE):
                pr.op("pe", lambda e, e_=e_, toff=toff, W=W: e.matmul(
                    pF[:, 3584:3584 + W], oneh[:, e_ * 128:(e_ + 1) * 128],
                    gfm[0:8, toff:toff + W], start=True, stop=True),
                    reads=["oneh", "gfm", "F7"], writes=["F7"])
                Gt = (P0 if e_ < 4 else P1)[:, (e_ % 4) * 512:(e_ % 4) * 512 + W]
                Gtn = ("P0" if e_ < 4 else "P1") + f".{e_ % 4}"
                pr.op("act", lambda e, W=W, Gt=Gt: e.activation(
                    Gt, pF[:, 3584:3584 + W], AF.Copy),
                    reads=["F7"], writes=[Gtn])
            for e_ in range(NE):
                ub = 4 * (e_ % 2)
                for k in range(4):
                    sv = sWeT[e_].rearrange("(dg p) j -> p dg j", p=128)
                    dvw = u[ub + k].rearrange("p (dg j) -> p dg j", j=1024)
                    dmaop("sp", lambda e, sv=sv, dvw=dvw, k=k: e.dma_start(
                        dvw, sv[:, 2 * k:2 * k + 2, :]),
                        f"u{ub + k}", reads=[f"sWeT{e_}"], writes=[f"u{ub + k}"])
                Gt = (P0 if e_ < 4 else P1)[:, (e_ % 4) * 512:(e_ % 4) * 512 + W]
                Gtn = ("P0" if e_ < 4 else "P1") + f".{e_ % 4}"
                for dc in range(DC):
                    rg, rgn = ring[dc % 4], f"ring{dc % 4}"
                    pr.op("dve", lambda e, dc=dc, toff=toff, W=W, rg=rg, Gt=Gt:
                          e.tensor_tensor(rg[:, 0:W],
                                          y[dc][:, toff:toff + W],
                                          Gt, ALU.mult),
                          reads=[f"y{dc}", Gtn], writes=[rgn])
                    for jc in range(DC):
                        r0 = jc * 512
                        pr.op("pe", lambda e, e_=e_, dc=dc, jc=jc, r0=r0,
                              W=W, rg=rg, ub=ub:
                              e.matmul(pF[:, r0:r0 + W],
                                       u[ub + dc // 2][:, (dc % 2) * 1024 +
                                                       jc * 128:(dc % 2) * 1024 +
                                                       (jc + 1) * 128],
                                       rg[:, 0:W],
                                       start=(e_ == 0 and dc == 0),
                                       stop=(e_ == NE - 1 and dc == DC - 1)),
                              reads=[f"u{ub + dc // 2}", rgn, f"F{jc}"],
                              writes=[f"F{jc}"])
            for jc in range(DC):
                r0 = jc * 512
                pr.op("dve", lambda e, jc=jc, toff=toff, W=W, r0=r0:
                      e.tensor_tensor(x[jc][:, toff:toff + W],
                                      x[jc][:, toff:toff + W],
                                      pF[:, r0:r0 + W], ALU.add),
                      reads=[f"x{jc}", f"F{jc}"], writes=[f"x{jc}"])

    gen_consts()
    if stage >= 2:
        prep_gate()

    for b in range(NB):
        x_in(b)
        if stage >= 1:
            ln()
        if b == 0 and stage >= 2:
            prep_ffn()
        if stage >= 1:
            attention("a")
        if stage >= 2:
            ln()
            ffn()
        if stage >= 3:
            ln(fp32_res=True)
            ln(src_is_y=True, affine=True)
            attention("b")
        if stage >= 4:
            ln()
            ffn()
        x_out(b)

    pr.finish()
    return nc


def _host_tables():
    half = DK // 2
    inv = 10000.0 ** (-np.arange(half, dtype=np.float64) * 2.0 / DK)
    ang = np.arange(SQ, dtype=np.float64)[None, :] * inv[:, None]
    cos = np.cos(ang).astype(np.float32)
    sin = np.sin(ang).astype(np.float32)
    T1 = np.zeros((128, SQ), np.float32)
    T2 = np.zeros((128, SQ), np.float32)
    for hb in (0, 64):
        T1[hb:hb + 32] = cos
        T1[hb + 32:hb + 64] = sin
        T2[hb:hb + 32] = sin
        T2[hb + 32:hb + 64] = cos
    Me = np.zeros((128, 128), np.float32)
    Mo = np.zeros((128, 128), np.float32)
    for hb in (0, 64):
        for j in range(32):
            Me[hb + j, hb + j] = 1.0
            Me[hb + 32 + j, hb + j] = -1.0
            Mo[hb + j, hb + 32 + j] = 1.0
            Mo[hb + 32 + j, hb + 32 + j] = 1.0
    return T1, T2, Me, Mo


def kernel(L0_attn_a_x, L0_attn_a_Wq, L0_attn_a_Wk, L0_attn_a_Wv, L0_attn_a_Wo,
           L0_attn_b_ln_w, L0_attn_b_ln_b, L0_attn_b_wq, L0_attn_b_wk,
           L0_attn_b_wv, L0_attn_b_wo, L0_ffn_w_gate, L0_ffn_w_in,
           L0_ffn_w_out):
    xfull = np.ascontiguousarray(np.asarray(L0_attn_a_x, dtype=np.float32))
    T1, T2, Me, Mo = _host_tables()
    perm = np.zeros(D, np.int64)
    for h in range(NH):
        for par in range(2):
            for j2 in range(32):
                perm[h * 64 + par * 32 + j2] = h * 64 + 2 * j2 + par
    common = {
        "aWqT": np.ascontiguousarray(
            np.asarray(L0_attn_a_Wq, dtype=np.float32)[perm].T),
        "aWkT": np.ascontiguousarray(
            np.asarray(L0_attn_a_Wk, dtype=np.float32)[perm].T),
        "aWvT": np.ascontiguousarray(
            np.asarray(L0_attn_a_Wv, dtype=np.float32).T),
        "aWoT": np.ascontiguousarray(
            np.asarray(L0_attn_a_Wo, dtype=np.float32).T),
        "blnw": np.ascontiguousarray(L0_attn_b_ln_w, dtype=np.float32),
        "blnb": np.ascontiguousarray(L0_attn_b_ln_b, dtype=np.float32),
        "bwq": np.ascontiguousarray(L0_attn_b_wq, dtype=np.float32),
        "bwk": np.ascontiguousarray(L0_attn_b_wk, dtype=np.float32),
        "bwv": np.ascontiguousarray(L0_attn_b_wv, dtype=np.float32),
        "bwo": np.ascontiguousarray(L0_attn_b_wo, dtype=np.float32),
        "fwg": np.ascontiguousarray(L0_ffn_w_gate, dtype=np.float32),
        "fwin": np.ascontiguousarray(L0_ffn_w_in, dtype=np.float32),
        "fwoutT": np.ascontiguousarray(
            np.asarray(L0_ffn_w_out, dtype=np.float32).transpose(0, 2, 1)),
        "tT1": T1, "tT2": T2, "tMe": Me, "tMo": Mo,
    }
    nc = build_nc(_STAGE)
    in_maps = []
    for c in range(NCORES):
        m = dict(common)
        m["x"] = xfull[c * NB:(c + 1) * NB]
        in_maps.append(m)
    res = run_bass_kernel_spmd(nc, in_maps, core_ids=list(range(NCORES)))
    return np.concatenate([r["out"] for r in res.results], axis=0)
